# version 23
# baseline (speedup 1.0000x reference)
import numpy as np
import ml_dtypes
from contextlib import ExitStack
import concourse.bass as bass
import concourse.mybir as mybir
from concourse.bass_utils import run_bass_kernel_spmd

F32 = mybir.dt.float32
BF16 = mybir.dt.bfloat16
U8 = mybir.dt.uint8
AF = mybir.ActivationFunctionType
ALU = mybir.AluOpType
AX = mybir.AxisListType
BFNP = ml_dtypes.bfloat16

NTOK = 4096
NQ = 2048
C_SHIFT = 20.0
SC = float(np.float32(1.0 / np.sqrt(np.float32(128.0))))
FWHT_SCALE = float(np.float32(1.0 / np.sqrt(8192.0)))
EPS_LN = 1e-5
EPS_RMS = 1e-6

ENGS = ("sync", "tensor", "scalar", "vector", "gpsimd")


class Rec:

    def __init__(self):
        self.ops = {e: [] for e in ENGS}
        self.cnt = {}
        self.pending = {}

    def tok(self, sem):
        return (sem, self.cnt.get(sem, 0))

    def op(self, eng, fn, waits=(), inc=None, dma=False, n=1):
        ws = list(self.pending.pop(eng, ())) + [w for w in waits if w is not None and w[1] > 0]
        amt = 0
        if inc is not None:
            amt = 16 * n if dma else n
            self.cnt[inc] = self.cnt.get(inc, 0) + amt
        self.ops[eng].append((fn, ws, inc, amt))
        return (inc, self.cnt[inc]) if inc is not None else None

    def barrier(self):
        bw = [(s, v) for s, v in self.cnt.items() if v > 0]
        self.pending = {e: list(bw) for e in ENGS}


def build_program(dbg=()):
    nc = bass.Bass("TRN2", target_bir_lowering=False)
    R = Rec()

    def scratch_kind(name):
        return "ExternalOutput" if name in dbg else "Internal"


    xb = nc.dram_tensor("xb", [NTOK, 2048], F32, kind="ExternalInput").ap()
    Wq = nc.dram_tensor("Wq", [2048, 4096], F32, kind="ExternalInput").ap()
    Wk = nc.dram_tensor("Wk", [2048, 4096], F32, kind="ExternalInput").ap()
    Wv = nc.dram_tensor("Wv", [2048, 2048], F32, kind="ExternalInput").ap()
    Wo = nc.dram_tensor("Wo", [2048, 2048], F32, kind="ExternalInput").ap()
    W1 = nc.dram_tensor("W1", [8192, 2048], F32, kind="ExternalInput").ap()
    W2 = nc.dram_tensor("W2", [2048, 8192], F32, kind="ExternalInput").ap()
    W3 = nc.dram_tensor("W3", [8192, 2048], F32, kind="ExternalInput").ap()
    gamma = nc.dram_tensor("gamma", [1, 2048], F32, kind="ExternalInput").ap()
    lam = nc.dram_tensor("lam", [1, 16], F32, kind="ExternalInput").ap()
    cOnesCol = nc.dram_tensor("cOnesCol", [128, 1], BF16, kind="ExternalInput").ap()
    cOnesRow = nc.dram_tensor("cOnesRow", [1, 128], BF16, kind="ExternalInput").ap()
    cOnesRowF = nc.dram_tensor("cOnesRowF", [1, 128], F32, kind="ExternalInput").ap()
    cIdent = nc.dram_tensor("cIdent", [128, 128], BF16, kind="ExternalInput").ap()
    cH128 = nc.dram_tensor("cH128", [128, 128], BF16, kind="ExternalInput").ap()
    cH64 = nc.dram_tensor("cH64", [64, 64], BF16, kind="ExternalInput").ap()
    yout = nc.dram_tensor("yout", [NQ, 2048], F32, kind="ExternalOutput").ap()

    W1bf = nc.dram_tensor("W1bf", [8192, 2048], BF16, kind=scratch_kind("W1bf")).ap()
    W3bf = nc.dram_tensor("W3bf", [8192, 2048], BF16, kind=scratch_kind("W3bf")).ap()
    W2bf = nc.dram_tensor("W2bf", [2048, 8192], BF16, kind=scratch_kind("W2bf")).ap()
    W1Ts = nc.dram_tensor("W1Ts", [64, 128, 16, 128], BF16, kind=scratch_kind("W1Ts")).ap()
    W3Ts = nc.dram_tensor("W3Ts", [64, 128, 16, 128], BF16, kind=scratch_kind("W3Ts")).ap()
    Y1s = nc.dram_tensor("Y1s", [64, 128, 2048], BF16, kind=scratch_kind("Y1s")).ap()
    W2fTs = nc.dram_tensor("W2fTs", [8192, 2048], BF16, kind=scratch_kind("W2fTs")).ap()
    Ks = nc.dram_tensor("Ks", [16, 2, 128, NTOK], BF16, kind=scratch_kind("Ks")).ap()
    Qs = nc.dram_tensor("Qs", [16, 2, 128, NQ], BF16, kind=scratch_kind("Qs")).ap()
    Vs = nc.dram_tensor("Vs", [16, 128, 32, 128], BF16, kind=scratch_kind("Vs")).ap()
    outsc = nc.dram_tensor("outsc", [16, 128, NQ], BF16, kind=scratch_kind("outsc")).ap()
    x2s = nc.dram_tensor("x2s", [NQ, 2048], F32, kind=scratch_kind("x2s")).ap()
    dbgXN = (nc.dram_tensor("dbgXN", [128, 16, NTOK], BF16, kind="ExternalOutput").ap()
             if "dbgXN" in dbg else None)
    dbgY = (nc.dram_tensor("dbgY", [NQ, 2048], F32, kind="ExternalOutput").ap()
            if "dbgY" in dbg else None)
    dbgNorm = (nc.dram_tensor("dbgNorm", [NQ // 128, 128, 16, 128], BF16, kind="ExternalOutput").ap()
               if "dbgY" in dbg else None)

    ctx = ExitStack()
    R1 = ctx.enter_context(nc.sbuf_tensor("R1", [128, 131072], U8))
    R2 = ctx.enter_context(nc.sbuf_tensor("R2", [128, 65536], U8))
    R3 = ctx.enter_context(nc.sbuf_tensor("R3", [128, 7296], U8))
    TRD = ctx.enter_context(nc.sbuf_tensor("TRD", [128, 8192], U8, is_transpose_dest=True))
    PS = ctx.enter_context(nc.psum_tensor("PS", [128, 4096], F32))

    def psb(bank, rows=128, width=512):
        return PS[0:rows, bank * 512: bank * 512 + width]

    PSbf = PS[:, :].bitcast(BF16)

    def rv(arena, off, shape, dtype=BF16, part=128):
        esz = 2 if dtype == BF16 else 4
        n = int(np.prod(shape)) * esz
        ap = arena[0:part, off:off + n].bitcast(dtype)
        if len(shape) == 2:
            ap = ap.rearrange("p (a b) -> p a b", a=shape[0], b=shape[1])
        elif len(shape) == 3:
            ap = ap.rearrange("p (a b c) -> p a b c", a=shape[0], b=shape[1], c=shape[2])
        return ap

    onesCol = rv(R3, 0, (1,))
    onesRow = rv(R3, 32, (128,), BF16, part=1)
    onesRowF = rv(R3, 6784, (128,), F32, part=1)
    ident = rv(R3, 288, (128,))
    h128 = rv(R3, 544, (128,))
    h64 = rv(R3, 800, (64,), BF16, part=64)
    lamrow = rv(R3, 928, (16,), F32, part=1)
    biasC = rv(R3, 992, (1,), F32)
    srow = rv(R3, 1024, (64,), F32)
    sc2 = rv(R3, 1280, (8,), F32, part=1)
    gcol = rv(R3, 1312, (16,))
    rinvb = rv(R3, 1344, (128,))
    epsln = rv(R3, 6720, (1,), F32)
    epsrms = rv(R3, 6752, (1,), F32)
    ssq = rv(R2, 32768, (2, NQ), F32, part=1)
    rrow = rv(R2, 49152, (3, 512), F32, part=1)
    rbsb = rv(R2, 55296, (2, 1024))
    r3bsb = rv(R2, 59392, (2, 512))
    ohst = rv(R2, 61440, (2, 512))
    sqb = rv(R2, 63488, (2, 512))
    rmsrow = rv(R1, 90112, (NQ,), F32, part=1)
    rinvbf = rv(R1, 126976, (NQ,), BF16, part=1)
    gamrow = rv(R1, 106496, (NQ,), F32, part=1)
    gamt = rv(R1, 114688, (NQ,), BF16, part=1)

    for dst, csrc in ((onesCol[:, :], cOnesCol), (onesRow[:, :], cOnesRow),
                      (onesRowF[:, :], cOnesRowF),
                      (ident[:, :], cIdent), (h128[:, :], cH128), (h64[:, :], cH64),
                      (lamrow[:, :], lam)):
        R.op("sync", (lambda d, s: (lambda h: h.dma_start(out=d, in_=s)))(dst, csrc),
             inc="dP0", dma=True)
    consts_tok = R.tok("dP0")
    cast_tasks = []
    tok_w2c = [None] * 4
    tok_w1c_ = [None] * 4
    tok_w3c_ = [None] * 4
    for j in range(4):
        cast_tasks.append(("w2", j, (lambda jj: (lambda h: h.dma_start(
            out=W2bf[:, jj * 2048:(jj + 1) * 2048], in_=W2[:, jj * 2048:(jj + 1) * 2048])))(j)))
    for j in range(4):
        cast_tasks.append(("w1", j, (lambda jj: (lambda h: h.dma_start(
            out=W1bf[jj * 2048:(jj + 1) * 2048, :], in_=W1[jj * 2048:(jj + 1) * 2048, :])))(j)))
    for j in range(4):
        cast_tasks.append(("w3", j, (lambda jj: (lambda h: h.dma_start(
            out=W3bf[jj * 2048:(jj + 1) * 2048, :], in_=W3[jj * 2048:(jj + 1) * 2048, :])))(j)))
    cast_idx = [0]

    def emit_cast(n):
        for _ in range(n):
            if cast_idx[0] >= len(cast_tasks):
                return
            kind, j, fn = cast_tasks[cast_idx[0]]
            t = R.op("gpsimd", fn, inc="dC_%s_%d" % (kind, j), dma=True)
            {"w2": tok_w2c, "w1": tok_w1c_, "w3": tok_w3c_}[kind][j] = t
            cast_idx[0] += 1
    tmemset = R.op("vector", lambda h: h.memset(biasC[:, :], -C_SHIFT), inc="tve")
    teps1 = R.op("vector", lambda h: h.memset(epsln[:, :], EPS_LN), inc="tve")
    teps2 = R.op("vector", lambda h: h.memset(epsrms[:, :], EPS_RMS), inc="tve")

    def ln_stats(x_ap, base, waits, stats=None):
        srow_ = stats if stats is not None else srow
        smean = srow_[:, base + 24:base + 25]
        scvar = srow_[:, base + 25:base + 26]
        smsq = srow_[:, base + 26:base + 27]
        mean = srow_[:, base + 27:base + 28]
        negm = srow_[:, base + 28:base + 29]
        v1 = srow_[:, base + 29:base + 30]
        v2 = srow_[:, base + 30:base + 31]
        v3 = srow[:, base + 24:base + 25]
        stdv = srow_[:, base + 25:base + 26]
        rstd = srow_[:, base + 26:base + 27]
        tb = None
        for c in range(4):
            tb = R.op("vector", (lambda a, b: (lambda h: h.bn_stats(a, b)))(
                srow_[:, base + 6 * c:base + 6 * c + 6], x_ap[:, c * 512:(c + 1) * 512]),
                waits=waits if c == 0 else (), inc="tve")
        means = srow_[:, base + 1:base + 24:3]
        cvars = srow_[:, base + 2:base + 24:3]
        t = R.op("vector", (lambda a, b: (lambda h: h.reduce_sum(a, b, axis=AX.X)))(smean, means),
                 waits=[tb], inc="tve")
        t = R.op("vector", (lambda a, b: (lambda h: h.reduce_sum(a, b, axis=AX.X)))(scvar, cvars),
                 waits=[t], inc="tve")
        t = R.op("vector", (lambda a, b, c_: (lambda h: h.scalar_tensor_tensor(
            c_, a, 1.0, a, ALU.mult, ALU.mult, accum_out=b)))(
                means, smsq, cvars), waits=[t], inc="tve")
        t = R.op("vector", (lambda a, b: (lambda h: h.tensor_scalar_mul(a, b, 1.0 / 8.0)))(
            mean, smean), waits=[t], inc="tve")
        t = R.op("vector", (lambda a, b: (lambda h: h.tensor_scalar_mul(a, b, -1.0 / 8.0)))(
            negm, smean), waits=[t], inc="tve")
        t = R.op("vector", (lambda a, b: (lambda h: h.tensor_scalar_mul(a, b, 1.0 / 2048.0)))(
            v1, scvar), waits=[t], inc="tve")
        t = R.op("vector", (lambda a, b, c_: (lambda h: h.scalar_tensor_tensor(
            a, b, 1.0 / 8.0, c_, ALU.mult, ALU.add)))(v2, smsq, v1), waits=[t], inc="tve")
        t = R.op("vector", (lambda a, b, c_: (lambda h: h.scalar_tensor_tensor(
            a, b, 1.0, c_, ALU.mult, ALU.mult)))(v1, mean, mean), waits=[t], inc="tve")
        t1b = R.op("vector", (lambda a, b, c_: (lambda h: h.tensor_sub(a, b, c_)))(v3, v2, v1),
                   waits=[t], inc="tve")
        ta = R.op("scalar", (lambda a, b: (lambda h: h.activation(a, b, AF.Sqrt,
            bias=epsln[:, :])))(stdv, v3), waits=[t1b, teps1], inc="tac")
        t2 = R.op("vector", (lambda a, b: (lambda h: h.reciprocal(a, b)))(rstd, stdv),
                  waits=[ta], inc="tve")
        return negm, rstd, t2

    xn1T = rv(R1, 0, (16, NTOK))
    xbt = [rv(R2, 8192 * i, (2048,), F32) for i in range(4)]
    xnt = [rv(R2, 32768 + 4096 * i, (2048,)) for i in range(4)]
    astats = [rv(R2, 49152 + 256 * i, (64,), F32) for i in range(4)]
    stt_toks = [None] * 4
    xnt_free = [None] * 4
    evA = [None, None]
    for i in range(32):
        p = i % 4
        ld = R.op("sync", (lambda pp, ii: (lambda h: h.dma_start(
            out=xbt[pp], in_=xb[ii * 128:(ii + 1) * 128, :])))(p, i),
            waits=[consts_tok, stt_toks[p]], inc="dA%d" % p, dma=True)
        if i % 3 == 2:
            emit_cast(1)
        negm, rstd, tln = ln_stats(xbt[p], 0, [ld], stats=astats[p])
        stt = R.op("vector", (lambda pp, nm, rs: (lambda h: h.tensor_scalar(
            xnt[pp], xbt[pp], nm, rs, ALU.add, ALU.mult)))(p, negm, rstd),
            waits=[xnt_free[p], tln], inc="tve")
        stt_toks[p] = stt
        for g in range(4):
            bslot = g % 2
            tp = None
            for sub in range(4):
                j = g * 4 + sub
                dst = PSbf[:, (6 + bslot) * 1024 + sub * 128:(6 + bslot) * 1024 + (sub + 1) * 128]
                tp = R.op("tensor", (lambda pp, jj, d: (lambda h: h.transpose(
                    d, xnt[pp][:, jj * 128:(jj + 1) * 128], ident[:, :])))(p, j, dst),
                    waits=[stt, evA[bslot]] if sub == 0 else (),
                    inc="tpe" if sub == 3 else None)
            ev = R.op("vector", (lambda ii, gg, sl: (lambda h: h.tensor_copy(
                xn1T[:, gg * 4:(gg + 1) * 4, ii * 128:(ii + 1) * 128],
                PSbf[:, (6 + sl) * 1024:(6 + sl) * 1024 + 512].rearrange(
                    "p (a b) -> p a b", a=4, b=128))))(i, g, bslot),
                waits=[tp], inc="tve")
            evA[bslot] = ev
            if g == 3:
                xnt_free[p] = tp
    R.barrier()

    if dbgXN is not None:
        R.op("sync", lambda h: h.dma_start(
            out=dbgXN.rearrange("p a b -> p (a b)"),
            in_=xn1T.rearrange("p a b -> p (a b)")), inc="dDbg", dma=True)
    wst = [rv(R2, 0, (16, 640)), rv(R2, 20480, (16, 640))]
    y1st = rv(R2, 57344, (2048,))
    trd = [TRD[:, 0:4096].bitcast(BF16), TRD[:, 4096:8192].bitcast(BF16)]
    st1_trd_free = [None, None]
    st1_bank = [None, None]
    st1_store = None
    qst = [rv(R3, 1600 + 1024 * s, (512,)) for s in range(3)]
    vtt = [rv(R3, 4672 + 1024 * s, (4, 128)) for s in range(2)]
    wfree = [None, None]
    qfree = [None] * 4
    v_sb_free = [None, None]
    v_ps_free = [None, None]
    bank_free = [None] * 4
    slot_ctr = [0]

    def proj_qk(h_, p, col_off, scr, tok_total):
        for cp in range(2):
            for tq in range(tok_total // 512):
                bank = slot_ctr[0] % 4
                s = slot_ctr[0] % 3
                last = None
                for i in range(16):
                    last = R.op("tensor", (lambda pp, c0, cpp, ii, tqq, bk: (lambda h: h.matmul(
                        psb(bk), wst[pp][:, ii, c0 + cpp * 128:c0 + (cpp + 1) * 128],
                        xn1T[:, ii, tqq * 512:(tqq + 1) * 512],
                        start=(ii == 0), stop=(ii == 15))))(p, col_off, cp, i, tq, bank),
                        waits=[R.tok("dW%d" % p), bank_free[bank]] if i == 0 else (),
                        inc="tpe" if i == 15 else None)
                ev = R.op("vector", (lambda bk, ss: (lambda h: h.tensor_copy(
                    qst[ss], psb(bk))))(bank, s), waits=[last, qfree[s]], inc="tve")
                bank_free[bank] = ev
                st = R.op("sync", (lambda hh, cpp, tqq, ss, sc: (lambda h: h.dma_start(
                    out=sc[hh, cpp, :, tqq * 512:(tqq + 1) * 512], in_=qst[ss])))(
                        h_, cp, tq, s, scr),
                    waits=[ev], inc="dS%d" % s, dma=True)
                qfree[s] = st
                slot_ctr[0] += 1

    for h_ in range(16):
        p = h_ % 2
        R.op("gpsimd", (lambda hh, pp: (lambda h: h.dma_start(
            out=wst[pp][:, :, 0:256],
            in_=Wq[:, hh * 256:(hh + 1) * 256].rearrange("(i dw) c -> dw i c", dw=128))))(h_, p),
            waits=[wfree[p], consts_tok], inc="dW%d" % p, dma=True)
        R.op("gpsimd", (lambda hh, pp: (lambda h: h.dma_start(
            out=wst[pp][:, :, 256:512],
            in_=Wk[:, hh * 256:(hh + 1) * 256].rearrange("(i dw) c -> dw i c", dw=128))))(h_, p),
            inc="dW%d" % p, dma=True)
        R.op("gpsimd", (lambda hh, pp: (lambda h: h.dma_start(
            out=wst[pp][:, :, 512:640],
            in_=Wv[:, hh * 128:(hh + 1) * 128].rearrange("(i dw) c -> dw i c", dw=128))))(h_, p),
            inc="dW%d" % p, dma=True)
        emit_cast(1 if h_ % 4 == 0 else 0)
        proj_qk(h_, p, 0, Qs, NQ)
        proj_qk(h_, p, 256, Ks, NTOK)
        vpend = [None]

        def v_transpose_block(tv, s, ev):
            vp = tv % 2
            tpt = None
            for j in range(4):
                dst = PSbf[:, 5 * 1024 + vp * 512 + j * 128: 5 * 1024 + vp * 512 + (j + 1) * 128]
                tpt = R.op("tensor", (lambda ss, jj, d: (lambda h: h.transpose(
                    d, qst[ss][:, jj * 128:(jj + 1) * 128], ident[:, :])))(s, j, dst),
                    waits=[ev, v_ps_free[vp]] if j == 0 else (),
                    inc="tpe" if j == 3 else None)
            ev2 = R.op("vector", (lambda vpp: (lambda h: h.tensor_copy(
                vtt[vpp][:, :, :],
                PSbf[:, 5 * 1024 + vpp * 512:5 * 1024 + vpp * 512 + 512].rearrange(
                    "p (a b) -> p a b", a=4, b=128))))(vp),
                waits=[tpt, v_sb_free[vp]], inc="tve")
            v_ps_free[vp] = ev2
            qfree[s] = ev2
            st = R.op("sync", (lambda hh, tvv, vpp: (lambda h: h.dma_start(
                out=Vs[hh, :, tvv * 4:(tvv + 1) * 4, :], in_=vtt[vpp][:, :, :])))(h_, tv, vp),
                waits=[ev2], inc="dVs%d" % vp, dma=True)
            v_sb_free[vp] = st

        for tv in range(8):
            bank = slot_ctr[0] % 4
            s = slot_ctr[0] % 3
            last = None
            for i in range(16):
                last = R.op("tensor", (lambda pp, ii, tvv, bk: (lambda h: h.matmul(
                    psb(bk), wst[pp][:, ii, 512:640],
                    xn1T[:, ii, tvv * 512:(tvv + 1) * 512],
                    start=(ii == 0), stop=(ii == 15))))(p, i, tv, bank),
                    waits=[R.tok("dW%d" % p), bank_free[bank]] if i == 0 else (),
                    inc="tpe" if i == 15 else None)
            ev = R.op("vector", (lambda bk, ss: (lambda h: h.tensor_copy(
                qst[ss], psb(bk))))(bank, s), waits=[last, qfree[s]], inc="tve")
            bank_free[bank] = ev
            slot_ctr[0] += 1
            if vpend[0] is not None:
                v_transpose_block(*vpend[0])
            vpend[0] = (tv, s, ev)
        v_transpose_block(*vpend[0])
        for c in range(4 * h_, 4 * h_ + 4):
            it = c % 2
            t_tr = R.op("sync", (lambda cc, itt: (lambda h: h.dma_start_transpose(
                trd[itt], W2bf[:, cc * 128:(cc + 1) * 128])))(c, it),
                waits=[tok_w2c[c // 16], st1_trd_free[it]], inc="dT2", dma=True)
            evs = None
            for n in range(4):
                bk = 6 + n % 2
                t_mm = R.op("tensor", (lambda itt, nn, bb: (lambda h: h.matmul(
                    psb(bb), h128[:, :], trd[itt][:, nn * 512:(nn + 1) * 512],
                    start=True, stop=True)))(it, n, bk),
                    waits=[t_tr, st1_bank[n % 2]], inc="tpe")
                evs = R.op("vector", (lambda nn, bb: (lambda h: h.tensor_copy(
                    y1st[:, nn * 512:(nn + 1) * 512], psb(bb))))(n, bk),
                    waits=[t_mm, st1_store] if n == 0 else [t_mm], inc="tve")
                st1_bank[n % 2] = evs
                if n == 3:
                    st1_trd_free[it] = t_mm
            st1_store = R.op("sync", (lambda cc: (lambda h: h.dma_start(
                out=Y1s[cc].rearrange("p d -> p d"), in_=y1st[:, :])))(c),
                waits=[evs], inc="dY1", dma=True)
        wfree[p] = R.tok("tpe")
    emit_cast(len(cast_tasks))
    R.barrier()

    ebuf = [rv(R1, 32768 * i, (32, 512)) for i in range(4)]
    kvb = dict(
        k1=rv(R2, 0, (4096,)),
        k2=rv(R2, 8192, (4096,)),
        v=rv(R2, 16384, (32, 128)),
        q1=rv(R2, 24576, (2048,)),
        q2=rv(R2, 28672, (2048,)),
    )
    tssq0 = R.op("vector", lambda h: h.memset(ssq[0:1, :, :], 0.0), inc="tve")

    units = [(h_, qs) for h_ in range(16) for qs in range(4)]
    NU = len(units)
    ut = [dict() for _ in range(NU)]
    PAIR_COL = [0, 1024, 3072]
    pf = [None, None, None]
    prot = [0]

    def next_pair():
        r = prot[0] % 3
        prot[0] += 1
        return r
    av_free = None
    oh_store = [None, None]
    sq_free = [None, None]
    ssq_add_prev = None
    sum1_rows_free = None
    sum2_rows_free = None
    sum3_rows_free = None
    lamb_free = [None, None]
    r1row_free = None
    r2row_free = None
    r3row_free = None
    kv_k1q1 = [None] * 17
    kv_k2q2 = [None] * 17
    kv_v = [None] * 17

    def load_k1q1(h_, waits):
        R.op("sync", (lambda hh: (lambda h: h.dma_start(
            out=kvb["k1"], in_=Ks[hh, 0, :, :])))(h_), waits=waits, inc="dKV0", dma=True)
        kv_k1q1[h_] = R.op("sync", (lambda hh: (lambda h: h.dma_start(
            out=kvb["q1"], in_=Qs[hh, 0, :, :])))(h_), inc="dKV0", dma=True)

    def load_k2q2(h_, waits_kq):
        R.op("sync", (lambda hh: (lambda h: h.dma_start(
            out=kvb["k2"], in_=Ks[hh, 1, :, :])))(h_), waits=waits_kq, inc="dKV1", dma=True)
        kv_k2q2[h_] = R.op("sync", (lambda hh: (lambda h: h.dma_start(
            out=kvb["q2"], in_=Qs[hh, 1, :, :])))(h_), inc="dKV1", dma=True)

    def load_v(h_, waits_v):
        kv_v[h_] = R.op("sync", (lambda hh: (lambda h: h.dma_start(
            out=kvb["v"].rearrange("p a b -> p (a b)"),
            in_=Vs[hh].rearrange("p a b -> p (a b)"))))(h_), waits=waits_v, inc="dKV2", dma=True)

    load_k1q1(0, [])
    load_k2q2(0, [])
    load_v(0, [])
    lamb_row = [None] * 16
    lamb = rv(R3, 1600, (2, 128), F32, part=1)


    wp_tasks = [(wsrc, wdst, (tok_w1c_ if wn == 0 else tok_w3c_)[hs * 2 // 2], hs, i)
                for wn, (wsrc, wdst) in enumerate(((W1bf, W1Ts), (W3bf, W3Ts)))
                for hs in range(4) for i in range(16)]
    wp_wr = [R.tok("tpe"), R.tok("tpe")]
    wp_idx = [0]

    def emit_wp(nmax):
        for _ in range(nmax):
            if wp_idx[0] >= len(wp_tasks):
                return
            wsrc, wdst, ctok_, hs, i = wp_tasks[wp_idx[0]]
            it = wp_idx[0] % 2
            tt_ = R.op("sync", (lambda ws, hss, ii, itt: (lambda h: h.dma_start_transpose(
                trd[itt], ws[hss * 2048:(hss + 1) * 2048, ii * 128:(ii + 1) * 128])))(
                    wsrc, hs, i, it),
                waits=[ctok_, wp_wr[it]], inc="dT%d" % it, dma=True)
            wp_wr[it] = R.op("sync", (lambda wd, hss, ii, itt: (lambda h: h.dma_start(
                out=wd[hss * 16:(hss + 1) * 16, :, ii, :].rearrange("j dw hh -> dw j hh"),
                in_=trd[itt].rearrange("dw (j hh) -> dw j hh", hh=128))))(wdst, hs, i, it),
                waits=[tt_], inc="dBW%d" % it, dma=True)
            wp_idx[0] += 1

    def s_block(u, which):
        h_, qs = units[u]
        E = ebuf[u % 4] if which == 1 else ebuf[(u + 3) % 4]
        ksl = kvb["k1"] if which == 1 else kvb["k2"]
        qsl = kvb["q1"] if which == 1 else kvb["q2"]
        sum_row = 0 if which == 1 else 32
        my_exp = []
        if which == 1:
            w_buf = ut[u - 2]["avstop"] if u >= 2 else None
            w_load = kv_k1q1[h_]
            sub_gate = None
        else:
            w_buf = None
            sub_gate = ut[u - 1]["diff_subs"] if u >= 1 else None
            w_load = kv_k2q2[h_]

        def emit_sums2_lagged(c0, c1):
            for c in range(c0, c1):
                wts = [sum2_rows_free, my_exp[0]] if c == 0 else []
                R.op("tensor", (lambda ccx, Eb, sr: (lambda h: h.matmul(
                    PS[sr:sr + 1, 2048:2560], onesCol[:, :], Eb[:, ccx, :],
                    start=(ccx == 0), stop=(ccx == 31))))(c, E, sum_row),
                    waits=wts, inc="tpe" if c == 31 else None)

        for g in range(16):
            r = next_pair()
            col = PAIR_COL[r]
            wts = [w_load] if g == 0 else []
            wts.append(pf[r])
            mmt = None
            for cc in range(2):
                c = 2 * g + cc
                mmt = R.op("tensor", (lambda ks, qs2, ccx, qss, cl: (lambda h: h.matmul(
                    PS[:, cl:cl + 512], ks[:, ccx * 128:(ccx + 1) * 128],
                    qs2[:, qss * 512:(qss + 1) * 512], start=True, stop=True)))(
                        ksl, qsl, c, qs, col + 512 * cc),
                    waits=wts if cc == 0 else (), inc="tpe" if cc == 1 else None)
            ew = [mmt]
            if g == 0:
                ew += [w_buf, tmemset]
            if sub_gate is not None and g % 4 == 0:
                ew.append(sub_gate[g // 4])
            my_exp.append(R.op("scalar", (lambda gg, cl, Eb: (lambda h: h.activation(
                Eb[:, 2 * gg:2 * gg + 2, :].rearrange("p a b -> p (a b)"),
                PS[:, cl:cl + 1024], AF.Exp, bias=biasC[:, :], scale=SC)))(
                    g, col, E), waits=ew, inc="tac"))
            pf[r] = my_exp[-1]
            if which == 2 and g >= 1:
                emit_sums2_lagged(2 * (g - 1), 2 * g)
        if which == 2:
            emit_sums2_lagged(30, 32)
        else:
            for c in range(32):
                wts = []
                if c == 0:
                    wts = [sum1_rows_free, my_exp[13]]
                elif c >= 28:
                    wts = [my_exp[c // 2]]
                R.op("tensor", (lambda ccx, Eb, sr: (lambda h: h.matmul(
                    PS[sr:sr + 1, 2048:2560], onesCol[:, :], Eb[:, ccx, :],
                    start=(ccx == 0), stop=(ccx == 31))))(c, E, sum_row),
                    waits=wts, inc="tpe" if c == 31 else None)
        ut[u]["exp1" if which == 1 else "exp2"] = my_exp
        ut[u]["sum%dstop" % which] = R.tok("tpe")

    def tail_diff(v):
        E1 = ebuf[v % 4]
        E2 = ebuf[(v + 3) % 4]
        T = ebuf[(v + 2) % 4]
        rp = v % 2
        e3prev = ut[v - 1]["e3_subs"] if v >= 1 else None
        diff_subs = []
        prev = None
        for s in range(4):
            t1 = R.op("vector", (lambda E1b, Tb, rpp, ss: (lambda h: h.tensor_tensor(
                Tb[:, 8 * ss:8 * ss + 8, :], E1b[:, 8 * ss:8 * ss + 8, :],
                rbsb[:, rpp, 0:512].unsqueeze(1).broadcast_to([128, 8, 512]), ALU.mult)))(
                    E1, T, rp, s),
                waits=([ut[v]["r1be"], prev] if s == 0 else [prev]) +
                      ([e3prev[s]] if e3prev is not None else []), inc="tve")
            t2 = R.op("vector", (lambda E1b, E2b, rpp, ss: (lambda h: h.tensor_tensor(
                E1b[:, 8 * ss:8 * ss + 8, :], E2b[:, 8 * ss:8 * ss + 8, :],
                rbsb[:, rpp, 512:1024].unsqueeze(1).broadcast_to([128, 8, 512]), ALU.mult)))(
                    E1, E2, rp, s),
                waits=[ut[v]["r2be"], t1] if s == 0 else [t1], inc="tve")
            td = R.op("vector", (lambda E1b, E2b, Tb, ss: (lambda h: h.tensor_tensor(
                E2b[:, 8 * ss:8 * ss + 8, :], Tb[:, 8 * ss:8 * ss + 8, :],
                E1b[:, 8 * ss:8 * ss + 8, :], ALU.subtract)))(E1, E2, T, s),
                waits=[t2], inc="tve")
            diff_subs.append(td)
            prev = td
        ut[v]["diff_subs"] = diff_subs

    def tail_e3(v):
        E2 = ebuf[(v + 3) % 4]
        T = ebuf[(v + 2) % 4]
        e3_subs = []
        for s in range(4):
            e3_subs.append(R.op("scalar", (lambda E2b, Tb, ss: (lambda h: h.activation(
                Tb[:, 8 * ss:8 * ss + 8, :].rearrange("p a b -> p (a b)"),
                E2b[:, 8 * ss:8 * ss + 8, :].rearrange("p a b -> p (a b)"), AF.Exp)))(E2, T, s),
                waits=[ut[v]["diff_subs"][s]], inc="tac"))
        ut[v]["e3_subs"] = e3_subs

    def tail_av(v):
        nonlocal sum3_rows_free
        h_, qs = units[v]
        T = ebuf[(v + 2) % 4]
        last_av = None
        for s in range(4):
            for c in range(8 * s, 8 * s + 8):
                wts = []
                if c == 8 * s:
                    wts.append(ut[v]["e3_subs"][s])
                    if s == 0:
                        wts += [av_free, kv_v[h_]]
                last_av = R.op("tensor", (lambda cc, Tb: (lambda h: h.matmul(
                    psb(5), kvb["v"][:, cc, :], Tb[:, cc, :],
                    start=(cc == 0), stop=(cc == 31))))(c, T),
                    waits=wts, inc="tpe" if c == 31 else None)
                R.op("tensor", (lambda cc, Tb: (lambda h: h.matmul(
                    PS[64:65, 2048:2560], onesCol[:, :], Tb[:, cc, :],
                    start=(cc == 0), stop=(cc == 31))))(c, T),
                    waits=[sum3_rows_free] if c == 0 else (),
                    inc="tpe" if c == 31 else None)
        ut[v]["avstop"] = last_av
        ut[v]["sum3stop"] = R.tok("tpe")

    def tail_finish(v):
        nonlocal av_free, ssq_add_prev, sum3_rows_free, r3row_free
        h_, qs = units[v]
        rp = v % 2
        r3r = rrow[0:1, 2, :]
        r = next_pair()
        lo = PAIR_COL[r]
        hi = lo + 512
        t_r3 = R.op("vector", (lambda rr: (lambda h: h.reciprocal(rr, PS[64:65, 2048:2560])))(r3r),
                    waits=[ut[v]["sum3stop"], r3row_free], inc="tve")
        sum3_rows_free = t_r3
        t_r3b = R.op("tensor", (lambda rr, cl: (lambda h: h.matmul(
            PS[:, cl:cl + 512], onesRowF[:, :], rr, start=True, stop=True)))(r3r, lo),
            waits=[t_r3, pf[r]], inc="tpe")
        r3row_free = t_r3b
        t_r3e = R.op("vector", (lambda rpp, cl: (lambda h: h.tensor_copy(
            r3bsb[:, rpp, :], PS[:, cl:cl + 512])))(rp, lo), waits=[t_r3b], inc="tve")
        t_oh = R.op("vector", (lambda rpp: (lambda h: h.tensor_tensor(
            ohst[:, rpp, :], psb(5), r3bsb[:, rpp, :], ALU.mult)))(rp),
            waits=[ut[v]["avstop"], t_r3e, oh_store[rp]], inc="tve")
        av_free = t_oh
        t_sq = R.op("vector", (lambda rpp: (lambda h: h.tensor_tensor(
            sqb[:, rpp, :], ohst[:, rpp, :], ohst[:, rpp, :], ALU.mult)))(rp),
            waits=[sq_free[rp], t_oh], inc="tve")
        t_ssqmm = R.op("tensor", (lambda rpp, cl: (lambda h: h.matmul(
            PS[0:1, cl:cl + 512], onesCol[:, :], sqb[:, rpp, :], start=True, stop=True)))(rp, hi),
            waits=[t_sq], inc="tpe")
        sq_free[rp] = t_ssqmm
        sr, dr = h_ % 2, (h_ + 1) % 2
        t_acc = R.op("vector", (lambda s_, d_, qss, cl: (lambda h: h.tensor_tensor(
            ssq[0:1, d_, qss * 512:(qss + 1) * 512], ssq[0:1, s_, qss * 512:(qss + 1) * 512],
            PS[0:1, cl:cl + 512], ALU.add)))(sr, dr, qs, hi),
            waits=[t_ssqmm, ssq_add_prev, tssq0], inc="tve")
        ssq_add_prev = t_acc
        pf[r] = t_acc
        st = R.op("sync", (lambda hh, qss, rpp: (lambda h: h.dma_start(
            out=outsc[hh, :, qss * 512:(qss + 1) * 512], in_=ohst[:, rpp, :])))(h_, qs, rp),
            waits=[t_oh, t_sq], inc="dOH%d" % rp, dma=True)
        oh_store[rp] = st
        ut[v]["done"] = R.tok("tpe")

    def recips_bcasts(u):
        nonlocal sum1_rows_free, sum2_rows_free, r1row_free, r2row_free
        rp = u % 2
        r1r = rrow[0:1, 0, :]
        r2r = rrow[0:1, 1, :]
        h_ = units[u][0]
        r = next_pair()
        lo = PAIR_COL[r]
        hi = lo + 512
        t_r1 = R.op("vector", (lambda rr: (lambda h: h.reciprocal(rr, PS[0:1, 2048:2560])))(r1r),
                    waits=[ut[u]["sum1stop"], r1row_free], inc="tve")
        sum1_rows_free = t_r1
        if qs == 0:
            lamb_row[h_] = R.op("vector", (lambda hh: (lambda h: h.tensor_scalar_mul(
                lamb[0:1, hh % 2, :], onesRowF[:, :], lamrow[0:1, hh:hh + 1])))(h_),
                waits=[lamb_free[h_ % 2]], inc="tve")
        t_r2 = R.op("vector", (lambda rr: (lambda h: h.reciprocal(rr, PS[32:33, 2048:2560])))(r2r),
                    waits=[ut[u]["sum2stop"], r2row_free], inc="tve")
        sum2_rows_free = t_r2
        t_r1b = R.op("tensor", (lambda rr, cl: (lambda h: h.matmul(
            PS[:, cl:cl + 512], onesRowF[:, :], rr, start=True, stop=True)))(r1r, lo),
            waits=[t_r1, pf[r]], inc="tpe")
        r1row_free = t_r1b
        t_r2b = R.op("tensor", (lambda rr, hh, cl: (lambda h: h.matmul(
            PS[:, cl:cl + 512], lamb[0:1, hh % 2, :], rr, start=True, stop=True)))(r2r, h_, hi),
            waits=[t_r2, lamb_row[h_]], inc="tpe")
        lamb_free[h_ % 2] = t_r2b
        r2row_free = t_r2b
        ut[u]["r1be"] = R.op("vector", (lambda rpp, cl: (lambda h: h.tensor_copy(
            rbsb[:, rpp, 0:512], PS[:, cl:cl + 512])))(rp, lo), waits=[t_r1b], inc="tve")
        ut[u]["r2be"] = R.op("vector", (lambda rpp, cl: (lambda h: h.tensor_copy(
            rbsb[:, rpp, 512:1024], PS[:, cl:cl + 512])))(rp, hi), waits=[t_r2b], inc="tve")
        pf[r] = ut[u]["r2be"]

    for u in range(NU):
        h_, qs = units[u]
        s_block(u, 1)
        if u >= 1:
            tail_diff(u - 1)
        s_block(u, 2)
        recips_bcasts(u)
        if u + 1 < NU and (u + 1) % 4 == 0:
            load_k1q1(h_ + 1, [ut[u]["sum1stop"]])
            load_k2q2(h_ + 1, [ut[u]["sum2stop"]])
        if u >= 1:
            tail_e3(u - 1)
            tail_av(u - 1)
            tail_finish(u - 1)
        if u % 4 == 0 and u >= 1:
            load_v(h_, [ut[u - 1]["avstop"]])
        emit_wp(2)
    emit_wp(len(wp_tasks))
    tail_diff(NU - 1)
    tail_e3(NU - 1)
    tail_av(NU - 1)
    tail_finish(NU - 1)
    R.barrier()

    wosb = rv(R1, 0, (16, 2048))
    outq = [rv(R1, 65536, (16, 128)), rv(R1, 69632, (16, 128))]
    ntmp = rv(R1, 73728, (16, 128))
    normed = rv(R1, 77824, (16, 128))
    xq = [rv(R1, 81920, (2048,), F32), rv(R1, 90112, (2048,), F32)]
    lnb = rv(R1, 98304, (2048,), F32)
    x2b = [rv(R1, 106496, (2048,), F32), rv(R1, 114688, (2048,), F32)]
    xn3st = rv(R1, 122880, (2048,))
    xn3T = rv(R2, 0, (16, NQ))

    b6_free = None
    b7_free = None
    t_rms = R.op("scalar", lambda h: h.activation(rmsrow[:, :], ssq[0:1, 0, :], AF.Sqrt,
                                                  bias=epsrms[0:1, :], scale=1.0 / 2048.0),
                 waits=[ssq_add_prev, teps2], inc="tac")
    t_rinv = R.op("vector", lambda h: h.reciprocal(rinvbf[:, :], rmsrow[:, :]),
                  waits=[t_rms], inc="tve")
    t_gl = R.op("sync", lambda h: h.dma_start(out=gamrow[:, :], in_=gamma),
                inc="dGam", dma=True)
    t_lm = R.op("vector", lambda h: h.reduce_sum(sc2[0:1, 0:1], lamrow[:, :], axis=AX.X),
                inc="tve")
    t_t = R.op("vector", lambda h: h.tensor_scalar(sc2[0:1, 1:2], sc2[0:1, 0:1], -1.0 / 16.0,
                                                   1.0, ALU.mult, ALU.add),
               waits=[t_lm], inc="tve")
    t_gt = R.op("vector", lambda h: h.tensor_scalar_mul(gamt[:, :], gamrow[:, :], sc2[0:1, 1:2]),
                waits=[t_t, t_gl], inc="tve")
    gevs = None
    gtp_free = [None, None]
    for i in range(16):
        k = i % 2
        dst = PSbf[:, 7 * 1024 + k * 512:7 * 1024 + k * 512 + 1]
        tp = R.op("tensor", (lambda ii, d: (lambda h: h.transpose(
            d, gamt[0:1, ii * 128:(ii + 1) * 128], ident[0:1, 0:1])))(i, dst),
            waits=[t_gt, gtp_free[k], b7_free], inc="tpe")
        gevs = R.op("vector", (lambda ii, d: (lambda h: h.tensor_copy(
            gcol[:, ii:ii + 1], d)))(i, dst), waits=[tp], inc="tve")
        gtp_free[k] = gevs
    t_wo = R.op("gpsimd", lambda h: h.dma_start(
        out=wosb[:, :, :], in_=Wo.rearrange("(i dw) c -> dw i c", dw=128)), inc="dWo", dma=True)

    Y1flat = Y1s.rearrange("c p d -> c (p d)")
    W2fflat = W2fTs.rearrange("(c p) d -> c (p d)", p=128)
    st2_in = [TRD[0:64, 0:4096].bitcast(BF16), TRD[0:64, 4096:8192].bitcast(BF16)]
    st2_out = rv(R3, 1600, (2048,), BF16, part=64)
    st2_mm_done = [None, None]
    st2_sc = None
    st2_st = [None, None]
    st2_ld = [None, None]
    st2_b7 = None

    def st2_load(i):
        it = i % 2
        st2_ld[it] = R.op("sync", (lambda jj, itt: (lambda h: h.dma_start(
            out=st2_in[itt], in_=Y1flat[:, jj * 2048:(jj + 1) * 2048])))(i, it),
            waits=[st2_mm_done[it]], inc="dY2L%d" % it, dma=True)

    def emit_st2(k):
        nonlocal st2_sc, st2_b7
        i = k // 2
        half = k % 2
        it = i % 2
        if half == 0 and i + 1 < 128:
            st2_load(i + 1)
        lastsc = None
        for n in (2 * half, 2 * half + 1):
            t_mm = R.op("tensor", (lambda itt, nn: (lambda h: h.matmul(
                PS[0:64, 3584:4096], h64[:, :], st2_in[itt][:, nn * 512:(nn + 1) * 512],
                start=True, stop=True)))(it, n),
                waits=([st2_ld[it], st2_sc] if n == 2 * half else [st2_sc]) +
                      ([gevs] if k == 0 else []),
                inc="tpe")
            lastsc = R.op("scalar", (lambda nn, hf: (lambda h: h.mul(
                st2_out[:, hf * 1024 + (nn % 2) * 512:hf * 1024 + (nn % 2 + 1) * 512],
                PS[0:64, 3584:4096], FWHT_SCALE)))(n, half),
                waits=[t_mm, st2_st[half] if n == 2 * half else None], inc="tac")
            st2_sc = lastsc
        st2_st[half] = R.op("sync", (lambda jj, hf: (lambda h: h.dma_start(
            out=W2fflat[:, jj * 2048 + hf * 1024:jj * 2048 + (hf + 1) * 1024],
            in_=st2_out[:, hf * 1024:(hf + 1) * 1024])))(i, half),
            waits=[lastsc], inc="dY2W%d" % half, dma=True)
        if half == 1:
            st2_mm_done[it] = st2_sc

    st2_load(0)
    oq_free = [None, None]
    xq_free = [None, None]
    x2w_done = [None, None]
    psy_free = None
    normed_free = None
    xn3_free = None
    trC_free = [None, None]
    for qt in range(16):
        for kk_ in range(16):
            emit_st2(qt * 16 + kk_)
        p = qt % 2
        oql = None
        for hh in range(16):
            oql = R.op("sync", (lambda hx, pp, qq: (lambda h: h.dma_start(
                out=outq[pp][:, hx, :], in_=outsc[hx, :, qq * 128:(qq + 1) * 128])))(hh, p, qt),
                waits=[oq_free[p]] if hh == 0 else (), inc="dOQ%d" % p, dma=True)
        xql = R.op("sync", (lambda pp, qq: (lambda h: h.dma_start(
            out=xq[pp], in_=xb[qq * 128:(qq + 1) * 128, :])))(p, qt),
            waits=[xq_free[p], t_rinv if qt < 2 else None], inc="dXQ%d" % p, dma=True)
        t_rb = R.op("tensor", (lambda qq: (lambda h: h.matmul(
            psb(6, 128, 128), onesRow[:, :], rinvbf[0:1, qq * 128:(qq + 1) * 128],
            start=True, stop=True)))(qt), waits=[t_rinv, b6_free], inc="tpe")
        t_rbe = R.op("vector", lambda h: h.tensor_copy(rinvb[:, :], psb(6, 128, 128)),
                     waits=[t_rb], inc="tve")
        b6_free = t_rbe
        t_n1 = R.op("vector", (lambda pp: (lambda h: h.tensor_tensor(
            ntmp[:, :, :], outq[pp][:, :, :],
            rinvb[:, :].unsqueeze(1).broadcast_to([128, 16, 128]), ALU.mult)))(p),
            waits=[oql, t_rbe, normed_free], inc="tve")
        t_n2 = R.op("vector", lambda h: h.tensor_tensor(
            normed[:, :, :], ntmp[:, :, :],
            gcol[:, :].unsqueeze(2).broadcast_to([128, 16, 128]), ALU.mult),
            waits=[t_n1, gevs], inc="tve")
        oq_free[p] = t_n1
        last = None
        for i in range(16):
            for nn in range(4):
                last = R.op("tensor", (lambda ii, nb: (lambda h: h.matmul(
                    PS[:, nb * 512:(nb + 1) * 512], normed[:, ii, :],
                    wosb[:, ii, nb * 512:(nb + 1) * 512],
                    start=(ii == 0), stop=(ii == 15))))(i, nn),
                    waits=[t_n2, t_wo, psy_free] if (i == 0 and nn == 0) else (),
                    inc="tpe" if (i == 15 and nn == 3) else None)
        normed_free = last
        negm2, rstd2, tst2 = ln_stats(PS[:, 0:2048], 0, [last])
        t_ln2 = R.op("vector", (lambda nm, rs: (lambda h: h.tensor_scalar(
            lnb, PS[:, 0:2048], nm, rs, ALU.add, ALU.mult)))(negm2, rstd2),
            waits=[tst2], inc="tve")
        psy_free = t_ln2
        if dbgY is not None:
            R.op("sync", (lambda qq: (lambda h: h.dma_start(
                out=dbgY[qq * 128:(qq + 1) * 128, :], in_=lnb)))(qt),
                waits=[t_ln2], inc="dDbg", dma=True)
            R.op("sync", (lambda qq: (lambda h: h.dma_start(
                out=dbgNorm[qq].rearrange("p a b -> p (a b)"),
                in_=normed.rearrange("p a b -> p (a b)"))))(qt),
                waits=[t_n2], inc="dDbg", dma=True)
        t_x2 = R.op("vector", (lambda pp: (lambda h: h.tensor_tensor(
            x2b[pp], xq[pp], lnb, ALU.add)))(p),
            waits=[t_ln2, xql, x2w_done[p], gevs if qt < 2 else None,
                   R.tok("dDbg") if dbgY is not None else None], inc="tve")
        xq_free[p] = t_x2
        t_x2w = R.op("gpsimd", (lambda pp, qq: (lambda h: h.dma_start(
            out=x2s[qq * 128:(qq + 1) * 128, :], in_=x2b[pp])))(p, qt),
            waits=[t_x2], inc="dX2W%d" % p, dma=True)
        x2w_done[p] = t_x2w
        negm3, rstd3, tst3 = ln_stats(x2b[p], 32, [t_x2])
        t_xn3 = R.op("vector", (lambda pp, nm, rs: (lambda h: h.tensor_scalar(
            xn3st, x2b[pp], nm, rs, ALU.add, ALU.mult)))(p, negm3, rstd3),
            waits=[tst3, xn3_free], inc="tve")
        for g in range(4):
            bslot = g % 2
            tp = None
            for sub in range(4):
                j = g * 4 + sub
                dst = PSbf[:, (4 + bslot) * 1024 + sub * 128:(4 + bslot) * 1024 + (sub + 1) * 128]
                tp = R.op("tensor", (lambda jj, d: (lambda h: h.transpose(
                    d, xn3st[:, jj * 128:(jj + 1) * 128], ident[:, :])))(j, dst),
                    waits=[t_xn3, trC_free[bslot]] if sub == 0 else (),
                    inc="tpe" if sub == 3 else None)
            if g == 3:
                xn3_free = tp
            ev = R.op("vector", (lambda qq, gg, sl: (lambda h: h.tensor_copy(
                xn3T[:, gg * 4:(gg + 1) * 4, qq * 128:(qq + 1) * 128],
                PSbf[:, (4 + sl) * 1024:(4 + sl) * 1024 + 512].rearrange(
                    "p (a b) -> p a b", a=4, b=128))))(qt, g, bslot),
                waits=[tp, t_rinv if (qt == 0 and g == 0) else None], inc="tve")
            trC_free[bslot] = ev
    R.barrier()

    hidden = rv(R1, 0, (64, 512))
    w1c = [rv(R1, 65536, (16, 128)), rv(R1, 69632, (16, 128))]
    w3c = [rv(R1, 73728, (16, 128)), rv(R1, 77824, (16, 128))]
    w2c = [rv(R1, 81920, (1024,)), rv(R1, 83968, (1024,))]
    slb = [rv(R1, 86016, (512,), F32), rv(R1, 88064, (512,), F32)]
    ybuf = [rv(R1, 90112, (2048,), F32), rv(R1, 98304, (2048,), F32),
            rv(R1, 65536, (2048,), F32), rv(R1, 73728, (2048,), F32)]
    x2q = [rv(R1, 106496, (2048,), F32), rv(R1, 114688, (2048,), F32)]
    lnb4 = rv(R1, 122880, (2048,), F32)


    w1_free = [None, None]
    w13_toks = [None, None]
    w2c_toks = [None, None]
    sil_free = [None, None]
    hid_free = None
    w1c_gate = None
    w3c_gate = None
    out01_gate = None
    w2_free = [None, None]
    x2q_free = [None, None]
    out_done = None
    y_evac = None
    out_toks = []
    for g in range(4):
        last_hid = None

        def load_w13(hcc):
            itt_ = hcc % 2
            R.op("sync", (lambda hx, itt: (lambda h: h.dma_start(
                out=w1c[itt][:, :, :].rearrange("p a b -> p (a b)"),
                in_=W1Ts[hx].rearrange("dw i hh -> dw (i hh)"))))(hcc, itt_),
                waits=[w1_free[itt_], w1c_gate if (hcc < 2 and g > 0) else None],
                inc="dF%d" % itt_, dma=True)
            w13_toks[itt_] = R.op("sync", (lambda hx, itt: (lambda h: h.dma_start(
                out=w3c[itt][:, :, :].rearrange("p a b -> p (a b)"),
                in_=W3Ts[hx].rearrange("dw i hh -> dw (i hh)"))))(hcc, itt_),
                waits=[w3c_gate if (hcc < 2 and g > 0) else None],
                inc="dF%d" % itt_, dma=True)

        load_w13(0)
        for hc in range(64):
            it = hc % 2
            bko = 2 * it
            if hc + 1 < 64:
                load_w13(hc + 1)
            t_l3 = w13_toks[it]
            lastg = None
            for i in range(16):
                lastg = R.op("tensor", (lambda ii, itt, gg, bk: (lambda h: h.matmul(
                    psb(bk), w1c[itt][:, ii, :], xn3T[:, ii, gg * 512:(gg + 1) * 512],
                    start=(ii == 0), stop=(ii == 15))))(i, it, g, bko),
                    waits=[t_l3, sil_free[it],
                           y_evac if (hc < 2 and g > 0) else None] if i == 0 else (),
                    inc="tpe" if i == 15 else None)
            lastl = None
            for i in range(16):
                lastl = R.op("tensor", (lambda ii, itt, gg, bk: (lambda h: h.matmul(
                    psb(bk), w3c[itt][:, ii, :], xn3T[:, ii, gg * 512:(gg + 1) * 512],
                    start=(ii == 0), stop=(ii == 15))))(i, it, g, bko + 1),
                    inc="tpe" if i == 15 else None)
            w1_free[it] = lastl
            t_sil = R.op("scalar", (lambda itt, bk: (lambda h: h.activation(
                slb[itt], psb(bk), AF.Silu)))(it, bko), waits=[lastg], inc="tac")
            last_hid = R.op("vector", (lambda hcc, itt, bk: (lambda h: h.tensor_tensor(
                hidden[:, hcc, :], slb[itt], psb(bk + 1), ALU.mult)))(hc, it, bko),
                waits=[t_sil, lastl, hid_free if hc < 2 else None], inc="tve")
            sil_free[it] = last_hid
        ev4 = [None] * 4
        for dh in range(2):
            lastmm = None

            def load_w2(hcc, dhh):
                itt_ = hcc % 2
                w2c_toks[itt_] = R.op("sync", (lambda hx, dx, itt: (lambda h: h.dma_start(
                    out=w2c[itt], in_=W2fTs[hx * 128:(hx + 1) * 128,
                                            dx * 1024:(dx + 1) * 1024])))(hcc, dhh, itt_),
                    waits=[w2_free[itt_],
                           R.tok("dY2W0") if (g == 0 and dhh == 0 and hcc < 2) else None,
                           R.tok("dY2W1") if (g == 0 and dhh == 0 and hcc < 2) else None],
                    inc="dG%d" % itt_, dma=True)

            load_w2(0, dh)
            for hc in range(64):
                it = hc % 2
                if hc + 1 < 64:
                    load_w2(hc + 1, dh)
                t_l2 = w2c_toks[it]
                for qt4 in range(4):
                    for nn in range(2):
                        ww = ()
                        if qt4 == 0 and nn == 0:
                            ww = [t_l2]
                            if hc == 0:
                                ww.append(last_hid)
                                if dh == 1:
                                    ww.append(ev4[3])
                                elif g > 0:
                                    ww.append(y_evac)
                        lastmm = R.op("tensor", (lambda hcc, itt, q4, nb: (lambda h: h.matmul(
                            PS[:, q4 * 1024 + nb * 512:q4 * 1024 + (nb + 1) * 512],
                            hidden[:, hcc, q4 * 128:(q4 + 1) * 128],
                            w2c[itt][:, nb * 512:(nb + 1) * 512],
                            start=(hcc == 0), stop=(hcc == 63))))(hc, it, qt4, nn),
                            waits=ww, inc="tpe" if (qt4 == 3 and nn == 1) else None)
                w2_free[it] = lastmm
            for qt4 in range(4):
                wv_ = [lastmm]
                if dh == 1:
                    wv_.append(ev4[qt4])
                elif g > 0:
                    wv_.append({2: w1c_gate, 3: w3c_gate}.get(qt4, out01_gate))
                ev4[qt4] = R.op("vector", (lambda q4, dhh: (lambda h: h.tensor_copy(
                    ybuf[q4][:, dhh * 1024:(dhh + 1) * 1024] if False else
                    ybuf[q4].rearrange("p n -> p n")[:, dhh * 1024:(dhh + 1) * 1024],
                    PS[:, q4 * 1024:(q4 + 1) * 1024])))(qt4, dh),
                    waits=wv_, inc="tve")
        hid_free = R.tok("tpe")
        ev4_last = ev4[3]
        ep_out = {}
        for qt4 in (2, 3, 0, 1):
            qt = g * 4 + qt4
            pq = qt4 % 2
            t_x2l = R.op("gpsimd", (lambda qq, pp: (lambda h: h.dma_start(
                out=x2q[pp], in_=x2s[qq * 128:(qq + 1) * 128, :])))(qt, pq),
                waits=[x2q_free[pq]], inc="dX2%d" % pq, dma=True)
            negm4, rstd4, tst4 = ln_stats(ybuf[qt4], 32 * (qt4 % 2), [ev4[qt4]])
            t_ln4 = R.op("vector", (lambda q4, nm, rs: (lambda h: h.tensor_scalar(
                lnb4, ybuf[q4], nm, rs, ALU.add, ALU.mult)))(qt4, negm4, rstd4),
                waits=[tst4], inc="tve")
            t_out = R.op("vector", (lambda q4, pp: (lambda h: h.tensor_tensor(
                ybuf[q4], lnb4, x2q[pp], ALU.add)))(qt4, pq),
                waits=[t_ln4, t_x2l, out_done], inc="tve")
            x2q_free[pq] = t_out
            ep_out[qt4] = t_out
            sem_out = {2: "dOUT2", 3: "dOUT3"}.get(qt4, "dOUT")
            out_done = R.op("gpsimd", (lambda qq, q4: (lambda h: h.dma_start(
                out=yout[qq * 128:(qq + 1) * 128, :], in_=ybuf[q4])))(qt, qt4),
                waits=[t_out], inc=sem_out, dma=True)
            ep_out[qt4] = out_done
            out_toks.append(out_done)
        y_evac = ev4_last
        w1c_gate = ep_out[2]
        w3c_gate = ep_out[3]
        out01_gate = ep_out[1]

    with ctx:
        needed = set(R.cnt.keys())
        for e in ENGS:
            for (fn, ws, inc, amt) in R.ops[e]:
                for s, v in ws:
                    needed.add(s)
        sem_handles = {name: ctx.enter_context(nc.semaphore(name)) for name in sorted(needed)}
        ctx.enter_context(nc.allow_low_precision(
            reason="bf16 matmul operands with fp32 accumulation by design"))
        with nc.Block() as block:
            def mk(engname):
                def runner(h):
                    for fn, ws, inc, amt in R.ops[engname]:
                        for s, v in ws:
                            h.wait_ge(sem_handles[s], v)
                        ins = fn(h)
                        if inc is not None:
                            ins.then_inc(sem_handles[inc], amt)
                    if engname == "gpsimd":
                        for snm in ("dOUT", "dOUT2", "dOUT3"):
                            if snm in R.cnt:
                                h.wait_ge(sem_handles[snm], R.cnt[snm])
                return runner
            block.sync(mk("sync"))
            block.tensor(mk("tensor"))
            block.scalar(mk("scalar"))
            block.vector(mk("vector"))
            block.gpsimd(mk("gpsimd"))
    return nc


_CONSTS = None


def _consts():
    global _CONSTS
    if _CONSTS is None:
        def hadamard(n):
            H = np.array([[1.0]], np.float32)
            while H.shape[0] < n:
                H = np.block([[H, H], [H, -H]])
            return H
        _CONSTS = {
            "cOnesCol": np.ones((128, 1), BFNP),
            "cOnesRow": np.ones((1, 128), BFNP),
            "cOnesRowF": np.ones((1, 128), np.float32),
            "cIdent": np.eye(128, dtype=BFNP),
            "cH128": hadamard(128).astype(BFNP),
            "cH64": hadamard(64).astype(BFNP),
        }
    return _CONSTS


def kernel(L0_attn_x, L0_attn_Wq, L0_attn_Wk, L0_attn_Wv, L0_attn_Wo,
           L0_attn_norm_scale, L0_attn_lam, L0_ffn_W1, L0_ffn_W2, L0_ffn_W3):
    x = np.ascontiguousarray(np.asarray(L0_attn_x, np.float32))
    shared = {
        "Wq": np.ascontiguousarray(L0_attn_Wq, np.float32),
        "Wk": np.ascontiguousarray(L0_attn_Wk, np.float32),
        "Wv": np.ascontiguousarray(L0_attn_Wv, np.float32),
        "Wo": np.ascontiguousarray(L0_attn_Wo, np.float32),
        "W1": np.ascontiguousarray(L0_ffn_W1, np.float32),
        "W2": np.ascontiguousarray(L0_ffn_W2, np.float32),
        "W3": np.ascontiguousarray(L0_ffn_W3, np.float32),
        "gamma": np.ascontiguousarray(np.asarray(L0_attn_norm_scale, np.float32).reshape(1, -1)),
        "lam": np.ascontiguousarray(np.asarray(L0_attn_lam, np.float32).reshape(1, -1)),
    }
    shared.update(_consts())
    in_maps = []
    for c in range(8):
        b, half = c // 2, c % 2
        own = x[b, half * 2048:(half + 1) * 2048]
        oth = x[b, (1 - half) * 2048:(2 - half) * 2048]
        m = dict(shared)
        m["xb"] = np.ascontiguousarray(np.concatenate([own, oth], 0))
        in_maps.append(m)
    nc = build_program()
    res = run_bass_kernel_spmd(nc, in_maps, core_ids=list(range(8)))
    out = np.empty((4, 4096, 2048), np.float32)
    for c in range(8):
        b, half = c // 2, c % 2
        out[b, half * 2048:(half + 1) * 2048] = np.asarray(res.results[c]["yout"], np.float32)
    return out


# version 24
# speedup vs baseline: 1.0698x; 1.0698x over previous
import numpy as np
import ml_dtypes
from contextlib import ExitStack
import concourse.bass as bass
import concourse.mybir as mybir
from concourse.bass_utils import run_bass_kernel_spmd

F32 = mybir.dt.float32
BF16 = mybir.dt.bfloat16
U8 = mybir.dt.uint8
AF = mybir.ActivationFunctionType
ALU = mybir.AluOpType
AX = mybir.AxisListType
BFNP = ml_dtypes.bfloat16

NTOK = 4096
NQ = 2048
C_SHIFT = 20.0
SC = float(np.float32(1.0 / np.sqrt(np.float32(128.0))))
FWHT_SCALE = float(np.float32(1.0 / np.sqrt(8192.0)))
EPS_LN = 1e-5
EPS_RMS = 1e-6

ENGS = ("sync", "tensor", "scalar", "vector", "gpsimd")


class Rec:

    def __init__(self):
        self.ops = {e: [] for e in ENGS}
        self.cnt = {}
        self.pending = {}

    def tok(self, sem):
        return (sem, self.cnt.get(sem, 0))

    def op(self, eng, fn, waits=(), inc=None, dma=False, n=1):
        ws = list(self.pending.pop(eng, ())) + [w for w in waits if w is not None and w[1] > 0]
        amt = 0
        if inc is not None:
            amt = 16 * n if dma else n
            self.cnt[inc] = self.cnt.get(inc, 0) + amt
        self.ops[eng].append((fn, ws, inc, amt))
        return (inc, self.cnt[inc]) if inc is not None else None

    def barrier(self):
        bw = [(s, v) for s, v in self.cnt.items() if v > 0]
        self.pending = {e: list(bw) for e in ENGS}


def build_program(dbg=()):
    nc = bass.Bass("TRN2", target_bir_lowering=False)
    R = Rec()

    def scratch_kind(name):
        return "ExternalOutput" if name in dbg else "Internal"


    xb = nc.dram_tensor("xb", [NTOK, 2048], F32, kind="ExternalInput").ap()
    Wq = nc.dram_tensor("Wq", [2048, 4096], F32, kind="ExternalInput").ap()
    Wk = nc.dram_tensor("Wk", [2048, 4096], F32, kind="ExternalInput").ap()
    Wv = nc.dram_tensor("Wv", [2048, 2048], F32, kind="ExternalInput").ap()
    Wo = nc.dram_tensor("Wo", [2048, 2048], F32, kind="ExternalInput").ap()
    W1 = nc.dram_tensor("W1", [8192, 2048], F32, kind="ExternalInput").ap()
    W2 = nc.dram_tensor("W2", [2048, 8192], F32, kind="ExternalInput").ap()
    W3 = nc.dram_tensor("W3", [8192, 2048], F32, kind="ExternalInput").ap()
    gamma = nc.dram_tensor("gamma", [1, 2048], F32, kind="ExternalInput").ap()
    lam = nc.dram_tensor("lam", [1, 16], F32, kind="ExternalInput").ap()
    cOnesCol = nc.dram_tensor("cOnesCol", [128, 1], BF16, kind="ExternalInput").ap()
    cOnesRow = nc.dram_tensor("cOnesRow", [1, 128], BF16, kind="ExternalInput").ap()
    cOnesRowF = nc.dram_tensor("cOnesRowF", [1, 128], F32, kind="ExternalInput").ap()
    cIdent = nc.dram_tensor("cIdent", [128, 128], BF16, kind="ExternalInput").ap()
    cH128 = nc.dram_tensor("cH128", [128, 128], BF16, kind="ExternalInput").ap()
    cH64 = nc.dram_tensor("cH64", [64, 64], BF16, kind="ExternalInput").ap()
    yout = nc.dram_tensor("yout", [NQ, 2048], F32, kind="ExternalOutput").ap()

    W1bf = nc.dram_tensor("W1bf", [8192, 2048], BF16, kind=scratch_kind("W1bf")).ap()
    W3bf = nc.dram_tensor("W3bf", [8192, 2048], BF16, kind=scratch_kind("W3bf")).ap()
    W2bf = nc.dram_tensor("W2bf", [2048, 8192], BF16, kind=scratch_kind("W2bf")).ap()
    W1Ts = nc.dram_tensor("W1Ts", [64, 128, 16, 128], BF16, kind=scratch_kind("W1Ts")).ap()
    W3Ts = nc.dram_tensor("W3Ts", [64, 128, 16, 128], BF16, kind=scratch_kind("W3Ts")).ap()
    Y1s = nc.dram_tensor("Y1s", [64, 128, 2048], BF16, kind=scratch_kind("Y1s")).ap()
    W2fTs = nc.dram_tensor("W2fTs", [8192, 2048], BF16, kind=scratch_kind("W2fTs")).ap()
    Ks = nc.dram_tensor("Ks", [16, 2, 128, NTOK], BF16, kind=scratch_kind("Ks")).ap()
    Qs = nc.dram_tensor("Qs", [16, 2, 128, NQ], BF16, kind=scratch_kind("Qs")).ap()
    Vs = nc.dram_tensor("Vs", [16, 128, 32, 128], BF16, kind=scratch_kind("Vs")).ap()
    outsc = nc.dram_tensor("outsc", [16, 128, NQ], BF16, kind=scratch_kind("outsc")).ap()
    x2s = nc.dram_tensor("x2s", [NQ, 2048], F32, kind=scratch_kind("x2s")).ap()
    dbgXN = (nc.dram_tensor("dbgXN", [128, 16, NTOK], BF16, kind="ExternalOutput").ap()
             if "dbgXN" in dbg else None)
    dbgY = (nc.dram_tensor("dbgY", [NQ, 2048], F32, kind="ExternalOutput").ap()
            if "dbgY" in dbg else None)
    dbgNorm = (nc.dram_tensor("dbgNorm", [NQ // 128, 128, 16, 128], BF16, kind="ExternalOutput").ap()
               if "dbgY" in dbg else None)

    ctx = ExitStack()
    R1 = ctx.enter_context(nc.sbuf_tensor("R1", [128, 131072], U8))
    R2 = ctx.enter_context(nc.sbuf_tensor("R2", [128, 65536], U8))
    R3 = ctx.enter_context(nc.sbuf_tensor("R3", [128, 7296], U8))
    TRD = ctx.enter_context(nc.sbuf_tensor("TRD", [128, 8192], U8, is_transpose_dest=True))
    PS = ctx.enter_context(nc.psum_tensor("PS", [128, 4096], F32))

    def psb(bank, rows=128, width=512):
        return PS[0:rows, bank * 512: bank * 512 + width]

    PSbf = PS[:, :].bitcast(BF16)

    def rv(arena, off, shape, dtype=BF16, part=128):
        esz = 2 if dtype == BF16 else 4
        n = int(np.prod(shape)) * esz
        ap = arena[0:part, off:off + n].bitcast(dtype)
        if len(shape) == 2:
            ap = ap.rearrange("p (a b) -> p a b", a=shape[0], b=shape[1])
        elif len(shape) == 3:
            ap = ap.rearrange("p (a b c) -> p a b c", a=shape[0], b=shape[1], c=shape[2])
        return ap

    onesCol = rv(R3, 0, (1,))
    onesRow = rv(R3, 32, (128,), BF16, part=1)
    onesRowF = rv(R3, 6784, (128,), F32, part=1)
    ident = rv(R3, 288, (128,))
    h128 = rv(R3, 544, (128,))
    h64 = rv(R3, 800, (64,), BF16, part=64)
    lamrow = rv(R3, 928, (16,), F32, part=1)
    biasC = rv(R3, 992, (1,), F32)
    srow = rv(R3, 1024, (64,), F32)
    sc2 = rv(R3, 1280, (8,), F32, part=1)
    gcol = rv(R3, 1312, (16,))
    rinvb = rv(R3, 1344, (128,))
    epsln = rv(R3, 6720, (1,), F32)
    epsrms = rv(R3, 6752, (1,), F32)
    ssq = rv(R2, 32768, (2, NQ), F32, part=1)
    rrow = rv(R2, 49152, (3, 512), F32, part=1)
    rbsb = rv(R2, 55296, (2, 1024))
    r3bsb = rv(R2, 59392, (2, 512))
    ohst = rv(R2, 61440, (2, 512))
    sqb = rv(R2, 63488, (2, 512))
    rmsrow = rv(R1, 90112, (NQ,), F32, part=1)
    rinvbf = rv(R1, 126976, (NQ,), BF16, part=1)
    gamrow = rv(R1, 106496, (NQ,), F32, part=1)
    gamt = rv(R1, 114688, (NQ,), BF16, part=1)

    for dst, csrc in ((onesCol[:, :], cOnesCol), (onesRow[:, :], cOnesRow),
                      (onesRowF[:, :], cOnesRowF),
                      (ident[:, :], cIdent), (h128[:, :], cH128), (h64[:, :], cH64),
                      (lamrow[:, :], lam)):
        R.op("sync", (lambda d, s: (lambda h: h.dma_start(out=d, in_=s)))(dst, csrc),
             inc="dP0", dma=True)
    consts_tok = R.tok("dP0")
    cast_tasks = []
    tok_w2c = [None] * 4
    tok_w1c_ = [None] * 4
    tok_w3c_ = [None] * 4
    for j in range(4):
        cast_tasks.append(("w2", j, (lambda jj: (lambda h: h.dma_start(
            out=W2bf[:, jj * 2048:(jj + 1) * 2048], in_=W2[:, jj * 2048:(jj + 1) * 2048])))(j)))
    for j in range(4):
        cast_tasks.append(("w1", j, (lambda jj: (lambda h: h.dma_start(
            out=W1bf[jj * 2048:(jj + 1) * 2048, :], in_=W1[jj * 2048:(jj + 1) * 2048, :])))(j)))
    for j in range(4):
        cast_tasks.append(("w3", j, (lambda jj: (lambda h: h.dma_start(
            out=W3bf[jj * 2048:(jj + 1) * 2048, :], in_=W3[jj * 2048:(jj + 1) * 2048, :])))(j)))
    cast_idx = [0]

    def emit_cast(n):
        for _ in range(n):
            if cast_idx[0] >= len(cast_tasks):
                return
            kind, j, fn = cast_tasks[cast_idx[0]]
            t = R.op("gpsimd", fn, inc="dC_%s_%d" % (kind, j), dma=True)
            {"w2": tok_w2c, "w1": tok_w1c_, "w3": tok_w3c_}[kind][j] = t
            cast_idx[0] += 1
    tmemset = R.op("vector", lambda h: h.memset(biasC[:, :], -C_SHIFT), inc="tve")
    teps1 = R.op("vector", lambda h: h.memset(epsln[:, :], EPS_LN), inc="tve")
    teps2 = R.op("vector", lambda h: h.memset(epsrms[:, :], EPS_RMS), inc="tve")

    def ln_stats(x_ap, base, waits, stats=None):
        srow_ = stats if stats is not None else srow
        smean = srow_[:, base + 24:base + 25]
        scvar = srow_[:, base + 25:base + 26]
        smsq = srow_[:, base + 26:base + 27]
        mean = srow_[:, base + 27:base + 28]
        negm = srow_[:, base + 28:base + 29]
        v1 = srow_[:, base + 29:base + 30]
        v2 = srow_[:, base + 30:base + 31]
        v3 = srow[:, base + 24:base + 25]
        stdv = srow_[:, base + 25:base + 26]
        rstd = srow_[:, base + 26:base + 27]
        tb = None
        for c in range(4):
            tb = R.op("vector", (lambda a, b: (lambda h: h.bn_stats(a, b)))(
                srow_[:, base + 6 * c:base + 6 * c + 6], x_ap[:, c * 512:(c + 1) * 512]),
                waits=waits if c == 0 else (), inc="tve")
        means = srow_[:, base + 1:base + 24:3]
        cvars = srow_[:, base + 2:base + 24:3]
        t = R.op("vector", (lambda a, b: (lambda h: h.reduce_sum(a, b, axis=AX.X)))(smean, means),
                 waits=[tb], inc="tve")
        t = R.op("vector", (lambda a, b: (lambda h: h.reduce_sum(a, b, axis=AX.X)))(scvar, cvars),
                 waits=[t], inc="tve")
        t = R.op("vector", (lambda a, b, c_: (lambda h: h.scalar_tensor_tensor(
            c_, a, 1.0, a, ALU.mult, ALU.mult, accum_out=b)))(
                means, smsq, cvars), waits=[t], inc="tve")
        t = R.op("vector", (lambda a, b: (lambda h: h.tensor_scalar_mul(a, b, 1.0 / 8.0)))(
            mean, smean), waits=[t], inc="tve")
        t = R.op("vector", (lambda a, b: (lambda h: h.tensor_scalar_mul(a, b, -1.0 / 8.0)))(
            negm, smean), waits=[t], inc="tve")
        t = R.op("vector", (lambda a, b: (lambda h: h.tensor_scalar_mul(a, b, 1.0 / 2048.0)))(
            v1, scvar), waits=[t], inc="tve")
        t = R.op("vector", (lambda a, b, c_: (lambda h: h.scalar_tensor_tensor(
            a, b, 1.0 / 8.0, c_, ALU.mult, ALU.add)))(v2, smsq, v1), waits=[t], inc="tve")
        t = R.op("vector", (lambda a, b, c_: (lambda h: h.scalar_tensor_tensor(
            a, b, 1.0, c_, ALU.mult, ALU.mult)))(v1, mean, mean), waits=[t], inc="tve")
        t1b = R.op("vector", (lambda a, b, c_: (lambda h: h.tensor_sub(a, b, c_)))(v3, v2, v1),
                   waits=[t], inc="tve")
        ta = R.op("scalar", (lambda a, b: (lambda h: h.activation(a, b, AF.Sqrt,
            bias=epsln[:, :])))(stdv, v3), waits=[t1b, teps1], inc="tac")
        t2 = R.op("vector", (lambda a, b: (lambda h: h.reciprocal(a, b)))(rstd, stdv),
                  waits=[ta], inc="tve")
        return negm, rstd, t2

    xn1T = rv(R1, 0, (16, NTOK))
    xbt = [rv(R2, 8192 * i, (2048,), F32) for i in range(4)]
    xnt = [rv(R2, 32768 + 4096 * i, (2048,)) for i in range(4)]
    astats = [rv(R2, 49152 + 256 * i, (64,), F32) for i in range(4)]
    stt_toks = [None] * 4
    xnt_free = [None] * 4
    evA = [None, None]
    for i in range(32):
        p = i % 4
        ld = R.op("sync", (lambda pp, ii: (lambda h: h.dma_start(
            out=xbt[pp], in_=xb[ii * 128:(ii + 1) * 128, :])))(p, i),
            waits=[consts_tok, stt_toks[p]], inc="dA%d" % p, dma=True)
        if i % 3 == 2:
            emit_cast(1)
        negm, rstd, tln = ln_stats(xbt[p], 0, [ld], stats=astats[p])
        stt = R.op("vector", (lambda pp, nm, rs: (lambda h: h.tensor_scalar(
            xnt[pp], xbt[pp], nm, rs, ALU.add, ALU.mult)))(p, negm, rstd),
            waits=[xnt_free[p], tln], inc="tve")
        stt_toks[p] = stt
        for g in range(4):
            bslot = g % 2
            tp = None
            for sub in range(4):
                j = g * 4 + sub
                dst = PSbf[:, (6 + bslot) * 1024 + sub * 128:(6 + bslot) * 1024 + (sub + 1) * 128]
                tp = R.op("tensor", (lambda pp, jj, d: (lambda h: h.transpose(
                    d, xnt[pp][:, jj * 128:(jj + 1) * 128], ident[:, :])))(p, j, dst),
                    waits=[stt, evA[bslot]] if sub == 0 else (),
                    inc="tpe" if sub == 3 else None)
            ev = R.op("vector", (lambda ii, gg, sl: (lambda h: h.tensor_copy(
                xn1T[:, gg * 4:(gg + 1) * 4, ii * 128:(ii + 1) * 128],
                PSbf[:, (6 + sl) * 1024:(6 + sl) * 1024 + 512].rearrange(
                    "p (a b) -> p a b", a=4, b=128))))(i, g, bslot),
                waits=[tp], inc="tve")
            evA[bslot] = ev
            if g == 3:
                xnt_free[p] = tp
    R.barrier()

    if dbgXN is not None:
        R.op("sync", lambda h: h.dma_start(
            out=dbgXN.rearrange("p a b -> p (a b)"),
            in_=xn1T.rearrange("p a b -> p (a b)")), inc="dDbg", dma=True)
    wst = [rv(R2, 0, (16, 640)), rv(R2, 20480, (16, 640))]
    y1st = rv(R2, 57344, (2048,))
    trd = [TRD[:, 0:4096].bitcast(BF16), TRD[:, 4096:8192].bitcast(BF16)]
    st1_trd_free = [None, None]
    st1_bank = [None, None]
    st1_store = None
    qst = [rv(R3, 1600 + 1024 * s, (512,)) for s in range(3)]
    vtt = [rv(R3, 4672 + 1024 * s, (4, 128)) for s in range(2)]
    wfree = [None, None]
    qfree = [None] * 4
    v_sb_free = [None, None]
    v_ps_free = [None, None]
    bank_free = [None] * 4
    slot_ctr = [0]

    def proj_qk(h_, p, col_off, scr, tok_total):
        for cp in range(2):
            for tq in range(tok_total // 512):
                bank = slot_ctr[0] % 4
                s = slot_ctr[0] % 3
                last = None
                for i in range(16):
                    last = R.op("tensor", (lambda pp, c0, cpp, ii, tqq, bk: (lambda h: h.matmul(
                        psb(bk), wst[pp][:, ii, c0 + cpp * 128:c0 + (cpp + 1) * 128],
                        xn1T[:, ii, tqq * 512:(tqq + 1) * 512],
                        start=(ii == 0), stop=(ii == 15))))(p, col_off, cp, i, tq, bank),
                        waits=[R.tok("dW%d" % p), bank_free[bank]] if i == 0 else (),
                        inc="tpe" if i == 15 else None)
                ev = R.op("vector", (lambda bk, ss: (lambda h: h.tensor_copy(
                    qst[ss], psb(bk))))(bank, s), waits=[last, qfree[s]], inc="tve")
                bank_free[bank] = ev
                st = R.op("sync", (lambda hh, cpp, tqq, ss, sc: (lambda h: h.dma_start(
                    out=sc[hh, cpp, :, tqq * 512:(tqq + 1) * 512], in_=qst[ss])))(
                        h_, cp, tq, s, scr),
                    waits=[ev], inc="dS%d" % s, dma=True)
                qfree[s] = st
                slot_ctr[0] += 1

    for h_ in range(16):
        p = h_ % 2
        R.op("gpsimd", (lambda hh, pp: (lambda h: h.dma_start(
            out=wst[pp][:, :, 0:256],
            in_=Wq[:, hh * 256:(hh + 1) * 256].rearrange("(i dw) c -> dw i c", dw=128))))(h_, p),
            waits=[wfree[p], consts_tok], inc="dW%d" % p, dma=True)
        R.op("gpsimd", (lambda hh, pp: (lambda h: h.dma_start(
            out=wst[pp][:, :, 256:512],
            in_=Wk[:, hh * 256:(hh + 1) * 256].rearrange("(i dw) c -> dw i c", dw=128))))(h_, p),
            inc="dW%d" % p, dma=True)
        R.op("gpsimd", (lambda hh, pp: (lambda h: h.dma_start(
            out=wst[pp][:, :, 512:640],
            in_=Wv[:, hh * 128:(hh + 1) * 128].rearrange("(i dw) c -> dw i c", dw=128))))(h_, p),
            inc="dW%d" % p, dma=True)
        emit_cast(1 if h_ % 4 == 0 else 0)
        proj_qk(h_, p, 0, Qs, NQ)
        proj_qk(h_, p, 256, Ks, NTOK)
        vpend = [None]

        def v_transpose_block(tv, s, ev):
            vp = tv % 2
            tpt = None
            for j in range(4):
                dst = PSbf[:, 5 * 1024 + vp * 512 + j * 128: 5 * 1024 + vp * 512 + (j + 1) * 128]
                tpt = R.op("tensor", (lambda ss, jj, d: (lambda h: h.transpose(
                    d, qst[ss][:, jj * 128:(jj + 1) * 128], ident[:, :])))(s, j, dst),
                    waits=[ev, v_ps_free[vp]] if j == 0 else (),
                    inc="tpe" if j == 3 else None)
            ev2 = R.op("vector", (lambda vpp: (lambda h: h.tensor_copy(
                vtt[vpp][:, :, :],
                PSbf[:, 5 * 1024 + vpp * 512:5 * 1024 + vpp * 512 + 512].rearrange(
                    "p (a b) -> p a b", a=4, b=128))))(vp),
                waits=[tpt, v_sb_free[vp]], inc="tve")
            v_ps_free[vp] = ev2
            qfree[s] = ev2
            st = R.op("sync", (lambda hh, tvv, vpp: (lambda h: h.dma_start(
                out=Vs[hh, :, tvv * 4:(tvv + 1) * 4, :], in_=vtt[vpp][:, :, :])))(h_, tv, vp),
                waits=[ev2], inc="dVs%d" % vp, dma=True)
            v_sb_free[vp] = st

        for tv in range(8):
            bank = slot_ctr[0] % 4
            s = slot_ctr[0] % 3
            last = None
            for i in range(16):
                last = R.op("tensor", (lambda pp, ii, tvv, bk: (lambda h: h.matmul(
                    psb(bk), wst[pp][:, ii, 512:640],
                    xn1T[:, ii, tvv * 512:(tvv + 1) * 512],
                    start=(ii == 0), stop=(ii == 15))))(p, i, tv, bank),
                    waits=[R.tok("dW%d" % p), bank_free[bank]] if i == 0 else (),
                    inc="tpe" if i == 15 else None)
            ev = R.op("vector", (lambda bk, ss: (lambda h: h.tensor_copy(
                qst[ss], psb(bk))))(bank, s), waits=[last, qfree[s]], inc="tve")
            bank_free[bank] = ev
            slot_ctr[0] += 1
            if vpend[0] is not None:
                v_transpose_block(*vpend[0])
            vpend[0] = (tv, s, ev)
        v_transpose_block(*vpend[0])
        for c in range(4 * h_, 4 * h_ + 4):
            it = c % 2
            t_tr = R.op("sync", (lambda cc, itt: (lambda h: h.dma_start_transpose(
                trd[itt], W2bf[:, cc * 128:(cc + 1) * 128])))(c, it),
                waits=[tok_w2c[c // 16], st1_trd_free[it]], inc="dT2", dma=True)
            evs = None
            for n in range(4):
                bk = 6 + n % 2
                t_mm = R.op("tensor", (lambda itt, nn, bb: (lambda h: h.matmul(
                    psb(bb), h128[:, :], trd[itt][:, nn * 512:(nn + 1) * 512],
                    start=True, stop=True)))(it, n, bk),
                    waits=[t_tr, st1_bank[n % 2]], inc="tpe")
                evs = R.op("vector", (lambda nn, bb: (lambda h: h.tensor_copy(
                    y1st[:, nn * 512:(nn + 1) * 512], psb(bb))))(n, bk),
                    waits=[t_mm, st1_store] if n == 0 else [t_mm], inc="tve")
                st1_bank[n % 2] = evs
                if n == 3:
                    st1_trd_free[it] = t_mm
            st1_store = R.op("sync", (lambda cc: (lambda h: h.dma_start(
                out=Y1s[cc].rearrange("p d -> p d"), in_=y1st[:, :])))(c),
                waits=[evs], inc="dY1", dma=True)
        wfree[p] = R.tok("tpe")
    emit_cast(len(cast_tasks))
    R.barrier()

    ebuf = [rv(R1, 32768 * i, (32, 512)) for i in range(4)]
    kvb = dict(
        k1=rv(R2, 0, (4096,)),
        k2=rv(R2, 8192, (4096,)),
        v=rv(R2, 16384, (32, 128)),
        q1=rv(R2, 24576, (2048,)),
        q2=rv(R2, 28672, (2048,)),
    )
    tssq0 = R.op("vector", lambda h: h.memset(ssq[0:1, :, :], 0.0), inc="tve")

    units = [(h_, qs) for h_ in range(16) for qs in range(4)]
    NU = len(units)
    ut = [dict() for _ in range(NU)]
    PAIR_COL = [0, 1024, 3072]
    pf = [None, None, None]
    prot = [0]

    def next_pair():
        r = prot[0] % 3
        prot[0] += 1
        return r
    av_free = None
    oh_store = [None, None]
    sq_free = [None, None]
    ssq_add_prev = None
    sum1_rows_free = None
    sum2_rows_free = None
    sum3_rows_free = None
    lamb_free = [None, None]
    r1row_free = None
    r2row_free = None
    r3row_free = None
    kv_k1q1 = [None] * 17
    kv_k2q2 = [None] * 17
    kv_v = [None] * 17

    def load_k1q1(h_, waits):
        R.op("sync", (lambda hh: (lambda h: h.dma_start(
            out=kvb["k1"], in_=Ks[hh, 0, :, :])))(h_), waits=waits, inc="dKV0", dma=True)
        kv_k1q1[h_] = R.op("sync", (lambda hh: (lambda h: h.dma_start(
            out=kvb["q1"], in_=Qs[hh, 0, :, :])))(h_), inc="dKV0", dma=True)

    def load_k2q2(h_, waits_kq):
        R.op("sync", (lambda hh: (lambda h: h.dma_start(
            out=kvb["k2"], in_=Ks[hh, 1, :, :])))(h_), waits=waits_kq, inc="dKV1", dma=True)
        kv_k2q2[h_] = R.op("sync", (lambda hh: (lambda h: h.dma_start(
            out=kvb["q2"], in_=Qs[hh, 1, :, :])))(h_), inc="dKV1", dma=True)

    def load_v(h_, waits_v):
        kv_v[h_] = R.op("sync", (lambda hh: (lambda h: h.dma_start(
            out=kvb["v"].rearrange("p a b -> p (a b)"),
            in_=Vs[hh].rearrange("p a b -> p (a b)"))))(h_), waits=waits_v, inc="dKV2", dma=True)

    load_k1q1(0, [])
    load_k2q2(0, [])
    load_v(0, [])
    lamb_row = [None] * 16
    lamb = rv(R3, 1600, (2, 128), F32, part=1)


    wp_tasks = [(wsrc, wdst, (tok_w1c_ if wn == 0 else tok_w3c_)[hs * 2 // 2], hs, i)
                for wn, (wsrc, wdst) in enumerate(((W1bf, W1Ts), (W3bf, W3Ts)))
                for hs in range(4) for i in range(16)]
    wp_wr = [R.tok("tpe"), R.tok("tpe")]
    wp_idx = [0]

    def emit_wp(nmax):
        for _ in range(nmax):
            if wp_idx[0] >= len(wp_tasks):
                return
            wsrc, wdst, ctok_, hs, i = wp_tasks[wp_idx[0]]
            it = wp_idx[0] % 2
            tt_ = R.op("sync", (lambda ws, hss, ii, itt: (lambda h: h.dma_start_transpose(
                trd[itt], ws[hss * 2048:(hss + 1) * 2048, ii * 128:(ii + 1) * 128])))(
                    wsrc, hs, i, it),
                waits=[ctok_, wp_wr[it]], inc="dT%d" % it, dma=True)
            wp_wr[it] = R.op("sync", (lambda wd, hss, ii, itt: (lambda h: h.dma_start(
                out=wd[hss * 16:(hss + 1) * 16, :, ii, :].rearrange("j dw hh -> dw j hh"),
                in_=trd[itt].rearrange("dw (j hh) -> dw j hh", hh=128))))(wdst, hs, i, it),
                waits=[tt_], inc="dBW%d" % it, dma=True)
            wp_idx[0] += 1

    def s_block(u, which):
        h_, qs = units[u]
        E = ebuf[u % 4] if which == 1 else ebuf[(u + 3) % 4]
        ksl = kvb["k1"] if which == 1 else kvb["k2"]
        qsl = kvb["q1"] if which == 1 else kvb["q2"]
        sum_row = 0 if which == 1 else 32
        my_exp = []
        if which == 1:
            w_buf = ut[u - 2]["avstop"] if u >= 2 else None
            w_load = kv_k1q1[h_]
            sub_gate = None
        else:
            w_buf = None
            sub_gate = ut[u - 1]["diff_subs"] if u >= 1 else None
            w_load = kv_k2q2[h_]

        def emit_sums2_lagged(c0, c1):
            for c in range(c0, c1):
                wts = [sum2_rows_free, my_exp[0]] if c == 0 else []
                R.op("tensor", (lambda ccx, Eb, sr: (lambda h: h.matmul(
                    PS[sr:sr + 1, 2048:2560], onesCol[:, :], Eb[:, ccx, :],
                    start=(ccx == 0), stop=(ccx == 31))))(c, E, sum_row),
                    waits=wts, inc="tpe" if c == 31 else None)

        for g in range(16):
            r = next_pair()
            col = PAIR_COL[r]
            wts = [w_load] if g == 0 else []
            wts.append(pf[r])
            mmt = None
            for cc in range(2):
                c = 2 * g + cc
                mmt = R.op("tensor", (lambda ks, qs2, ccx, qss, cl: (lambda h: h.matmul(
                    PS[:, cl:cl + 512], ks[:, ccx * 128:(ccx + 1) * 128],
                    qs2[:, qss * 512:(qss + 1) * 512], start=True, stop=True)))(
                        ksl, qsl, c, qs, col + 512 * cc),
                    waits=wts if cc == 0 else (), inc="tpe" if cc == 1 else None)
            ew = [mmt]
            if g == 0:
                ew += [w_buf, tmemset]
            if sub_gate is not None and g % 4 == 0:
                ew.append(sub_gate[g // 4])
            my_exp.append(R.op("scalar", (lambda gg, cl, Eb: (lambda h: h.activation(
                Eb[:, 2 * gg:2 * gg + 2, :].rearrange("p a b -> p (a b)"),
                PS[:, cl:cl + 1024], AF.Exp, bias=biasC[:, :], scale=SC)))(
                    g, col, E), waits=ew, inc="tac"))
            pf[r] = my_exp[-1]
            if which == 2 and g >= 1:
                emit_sums2_lagged(2 * (g - 1), 2 * g)
        if which == 2:
            emit_sums2_lagged(30, 32)
        else:
            for c in range(32):
                wts = []
                if c == 0:
                    wts = [sum1_rows_free, my_exp[13]]
                elif c >= 28:
                    wts = [my_exp[c // 2]]
                R.op("tensor", (lambda ccx, Eb, sr: (lambda h: h.matmul(
                    PS[sr:sr + 1, 2048:2560], onesCol[:, :], Eb[:, ccx, :],
                    start=(ccx == 0), stop=(ccx == 31))))(c, E, sum_row),
                    waits=wts, inc="tpe" if c == 31 else None)
        ut[u]["exp1" if which == 1 else "exp2"] = my_exp
        ut[u]["sum%dstop" % which] = R.tok("tpe")

    def tail_diff(v):
        E1 = ebuf[v % 4]
        E2 = ebuf[(v + 3) % 4]
        T = ebuf[(v + 2) % 4]
        rp = v % 2
        e3prev = ut[v - 1]["e3_subs"] if v >= 1 else None
        diff_subs = []
        prev = None
        for s in range(4):
            t1 = R.op("vector", (lambda E1b, Tb, rpp, ss: (lambda h: h.tensor_tensor(
                Tb[:, 8 * ss:8 * ss + 8, :], E1b[:, 8 * ss:8 * ss + 8, :],
                rbsb[:, rpp, 0:512].unsqueeze(1).broadcast_to([128, 8, 512]), ALU.mult)))(
                    E1, T, rp, s),
                waits=([ut[v]["r1be"], prev] if s == 0 else [prev]) +
                      ([e3prev[s]] if e3prev is not None else []), inc="tve")
            t2 = R.op("vector", (lambda E1b, E2b, rpp, ss: (lambda h: h.tensor_tensor(
                E1b[:, 8 * ss:8 * ss + 8, :], E2b[:, 8 * ss:8 * ss + 8, :],
                rbsb[:, rpp, 512:1024].unsqueeze(1).broadcast_to([128, 8, 512]), ALU.mult)))(
                    E1, E2, rp, s),
                waits=[ut[v]["r2be"], t1] if s == 0 else [t1], inc="tve")
            td = R.op("vector", (lambda E1b, E2b, Tb, ss: (lambda h: h.tensor_tensor(
                E2b[:, 8 * ss:8 * ss + 8, :], Tb[:, 8 * ss:8 * ss + 8, :],
                E1b[:, 8 * ss:8 * ss + 8, :], ALU.subtract)))(E1, E2, T, s),
                waits=[t2], inc="tve")
            diff_subs.append(td)
            prev = td
        ut[v]["diff_subs"] = diff_subs

    def tail_e3(v):
        E2 = ebuf[(v + 3) % 4]
        T = ebuf[(v + 2) % 4]
        e3_subs = []
        for s in range(4):
            e3_subs.append(R.op("scalar", (lambda E2b, Tb, ss: (lambda h: h.activation(
                Tb[:, 8 * ss:8 * ss + 8, :].rearrange("p a b -> p (a b)"),
                E2b[:, 8 * ss:8 * ss + 8, :].rearrange("p a b -> p (a b)"), AF.Exp)))(E2, T, s),
                waits=[ut[v]["diff_subs"][s]], inc="tac"))
        ut[v]["e3_subs"] = e3_subs

    def tail_av(v):
        nonlocal sum3_rows_free
        h_, qs = units[v]
        T = ebuf[(v + 2) % 4]
        last_av = None
        for s in range(4):
            for c in range(8 * s, 8 * s + 8):
                wts = []
                if c == 8 * s:
                    wts.append(ut[v]["e3_subs"][s])
                    if s == 0:
                        wts += [av_free, kv_v[h_]]
                last_av = R.op("tensor", (lambda cc, Tb: (lambda h: h.matmul(
                    psb(5), kvb["v"][:, cc, :], Tb[:, cc, :],
                    start=(cc == 0), stop=(cc == 31))))(c, T),
                    waits=wts, inc="tpe" if c == 31 else None)
                R.op("tensor", (lambda cc, Tb: (lambda h: h.matmul(
                    PS[64:65, 2048:2560], onesCol[:, :], Tb[:, cc, :],
                    start=(cc == 0), stop=(cc == 31))))(c, T),
                    waits=[sum3_rows_free] if c == 0 else (),
                    inc="tpe" if c == 31 else None)
        ut[v]["avstop"] = last_av
        ut[v]["sum3stop"] = R.tok("tpe")

    def tail_finish(v):
        nonlocal av_free, ssq_add_prev, sum3_rows_free, r3row_free
        h_, qs = units[v]
        rp = v % 2
        r3r = rrow[0:1, 2, :]
        r = next_pair()
        lo = PAIR_COL[r]
        hi = lo + 512
        t_r3 = R.op("vector", (lambda rr: (lambda h: h.reciprocal(rr, PS[64:65, 2048:2560])))(r3r),
                    waits=[ut[v]["sum3stop"], r3row_free], inc="tve")
        sum3_rows_free = t_r3
        t_r3b = R.op("tensor", (lambda rr, cl: (lambda h: h.matmul(
            PS[:, cl:cl + 512], onesRowF[:, :], rr, start=True, stop=True)))(r3r, lo),
            waits=[t_r3, pf[r]], inc="tpe")
        r3row_free = t_r3b
        t_r3e = R.op("vector", (lambda rpp, cl: (lambda h: h.tensor_copy(
            r3bsb[:, rpp, :], PS[:, cl:cl + 512])))(rp, lo), waits=[t_r3b], inc="tve")
        t_oh = R.op("vector", (lambda rpp: (lambda h: h.tensor_tensor(
            ohst[:, rpp, :], psb(5), r3bsb[:, rpp, :], ALU.mult)))(rp),
            waits=[ut[v]["avstop"], t_r3e, oh_store[rp]], inc="tve")
        av_free = t_oh
        t_sq = R.op("vector", (lambda rpp: (lambda h: h.tensor_tensor(
            sqb[:, rpp, :], ohst[:, rpp, :], ohst[:, rpp, :], ALU.mult)))(rp),
            waits=[sq_free[rp], t_oh], inc="tve")
        t_ssqmm = R.op("tensor", (lambda rpp, cl: (lambda h: h.matmul(
            PS[0:1, cl:cl + 512], onesCol[:, :], sqb[:, rpp, :], start=True, stop=True)))(rp, hi),
            waits=[t_sq], inc="tpe")
        sq_free[rp] = t_ssqmm
        sr, dr = h_ % 2, (h_ + 1) % 2
        t_acc = R.op("vector", (lambda s_, d_, qss, cl: (lambda h: h.tensor_tensor(
            ssq[0:1, d_, qss * 512:(qss + 1) * 512], ssq[0:1, s_, qss * 512:(qss + 1) * 512],
            PS[0:1, cl:cl + 512], ALU.add)))(sr, dr, qs, hi),
            waits=[t_ssqmm, ssq_add_prev, tssq0], inc="tve")
        ssq_add_prev = t_acc
        pf[r] = t_acc
        st = R.op("sync", (lambda hh, qss, rpp: (lambda h: h.dma_start(
            out=outsc[hh, :, qss * 512:(qss + 1) * 512], in_=ohst[:, rpp, :])))(h_, qs, rp),
            waits=[t_oh, t_sq], inc="dOH%d" % rp, dma=True)
        oh_store[rp] = st
        ut[v]["done"] = R.tok("tpe")

    def recips_bcasts(u):
        nonlocal sum1_rows_free, sum2_rows_free, r1row_free, r2row_free
        rp = u % 2
        r1r = rrow[0:1, 0, :]
        r2r = rrow[0:1, 1, :]
        h_ = units[u][0]
        r = next_pair()
        lo = PAIR_COL[r]
        hi = lo + 512
        t_r1 = R.op("vector", (lambda rr: (lambda h: h.reciprocal(rr, PS[0:1, 2048:2560])))(r1r),
                    waits=[ut[u]["sum1stop"], r1row_free], inc="tve")
        sum1_rows_free = t_r1
        if qs == 0:
            lamb_row[h_] = R.op("vector", (lambda hh: (lambda h: h.tensor_scalar_mul(
                lamb[0:1, hh % 2, :], onesRowF[:, :], lamrow[0:1, hh:hh + 1])))(h_),
                waits=[lamb_free[h_ % 2]], inc="tve")
        t_r2 = R.op("vector", (lambda rr: (lambda h: h.reciprocal(rr, PS[32:33, 2048:2560])))(r2r),
                    waits=[ut[u]["sum2stop"], r2row_free], inc="tve")
        sum2_rows_free = t_r2
        t_r1b = R.op("tensor", (lambda rr, cl: (lambda h: h.matmul(
            PS[:, cl:cl + 512], onesRowF[:, :], rr, start=True, stop=True)))(r1r, lo),
            waits=[t_r1, pf[r]], inc="tpe")
        r1row_free = t_r1b
        t_r2b = R.op("tensor", (lambda rr, hh, cl: (lambda h: h.matmul(
            PS[:, cl:cl + 512], lamb[0:1, hh % 2, :], rr, start=True, stop=True)))(r2r, h_, hi),
            waits=[t_r2, lamb_row[h_]], inc="tpe")
        lamb_free[h_ % 2] = t_r2b
        r2row_free = t_r2b
        ut[u]["r1be"] = R.op("vector", (lambda rpp, cl: (lambda h: h.tensor_copy(
            rbsb[:, rpp, 0:512], PS[:, cl:cl + 512])))(rp, lo), waits=[t_r1b], inc="tve")
        ut[u]["r2be"] = R.op("vector", (lambda rpp, cl: (lambda h: h.tensor_copy(
            rbsb[:, rpp, 512:1024], PS[:, cl:cl + 512])))(rp, hi), waits=[t_r2b], inc="tve")
        pf[r] = ut[u]["r2be"]

    for u in range(NU):
        h_, qs = units[u]
        s_block(u, 1)
        if u >= 1:
            tail_diff(u - 1)
        s_block(u, 2)
        recips_bcasts(u)
        if u + 1 < NU and (u + 1) % 4 == 0:
            load_k1q1(h_ + 1, [ut[u]["sum1stop"]])
            load_k2q2(h_ + 1, [ut[u]["sum2stop"]])
        if u >= 1:
            tail_e3(u - 1)
            tail_av(u - 1)
            tail_finish(u - 1)
        if u % 4 == 0 and u >= 1:
            load_v(h_, [ut[u - 1]["avstop"]])
        emit_wp(2)
    emit_wp(len(wp_tasks))
    tail_diff(NU - 1)
    tail_e3(NU - 1)
    tail_av(NU - 1)
    tail_finish(NU - 1)
    R.barrier()

    wosb = rv(R1, 0, (16, 2048))
    outq = [rv(R1, 65536, (16, 128)), rv(R1, 69632, (16, 128))]
    ntmp = rv(R1, 73728, (16, 128))
    normed = rv(R1, 77824, (16, 128))
    xq = [rv(R1, 81920, (2048,), F32), rv(R1, 90112, (2048,), F32)]
    lnb = rv(R1, 98304, (2048,), F32)
    x2b = [rv(R1, 106496, (2048,), F32), rv(R1, 114688, (2048,), F32)]
    xn3st = rv(R1, 122880, (2048,))
    xn3T = rv(R2, 0, (16, NQ))

    b6_free = None
    b7_free = None
    t_rms = R.op("scalar", lambda h: h.activation(rmsrow[:, :], ssq[0:1, 0, :], AF.Sqrt,
                                                  bias=epsrms[0:1, :], scale=1.0 / 2048.0),
                 waits=[ssq_add_prev, teps2], inc="tac")
    t_rinv = R.op("vector", lambda h: h.reciprocal(rinvbf[:, :], rmsrow[:, :]),
                  waits=[t_rms], inc="tve")
    t_gl = R.op("sync", lambda h: h.dma_start(out=gamrow[:, :], in_=gamma),
                inc="dGam", dma=True)
    t_lm = R.op("vector", lambda h: h.reduce_sum(sc2[0:1, 0:1], lamrow[:, :], axis=AX.X),
                inc="tve")
    t_t = R.op("vector", lambda h: h.tensor_scalar(sc2[0:1, 1:2], sc2[0:1, 0:1], -1.0 / 16.0,
                                                   1.0, ALU.mult, ALU.add),
               waits=[t_lm], inc="tve")
    t_gt = R.op("vector", lambda h: h.tensor_scalar_mul(gamt[:, :], gamrow[:, :], sc2[0:1, 1:2]),
                waits=[t_t, t_gl], inc="tve")
    gevs = None
    gtp_free = [None, None]
    for i in range(16):
        k = i % 2
        dst = PSbf[:, 7 * 1024 + k * 512:7 * 1024 + k * 512 + 1]
        tp = R.op("tensor", (lambda ii, d: (lambda h: h.transpose(
            d, gamt[0:1, ii * 128:(ii + 1) * 128], ident[0:1, 0:1])))(i, dst),
            waits=[t_gt, gtp_free[k], b7_free], inc="tpe")
        gevs = R.op("vector", (lambda ii, d: (lambda h: h.tensor_copy(
            gcol[:, ii:ii + 1], d)))(i, dst), waits=[tp], inc="tve")
        gtp_free[k] = gevs
    t_wo = R.op("gpsimd", lambda h: h.dma_start(
        out=wosb[:, :, :], in_=Wo.rearrange("(i dw) c -> dw i c", dw=128)), inc="dWo", dma=True)

    Y1flat = Y1s.rearrange("c p d -> c (p d)")
    W2fflat = W2fTs.rearrange("(c p) d -> c (p d)", p=128)
    st2_in = [TRD[0:64, 0:4096].bitcast(BF16), TRD[0:64, 4096:8192].bitcast(BF16)]
    st2_out = rv(R3, 1600, (2048,), BF16, part=64)
    st2_mm_done = [None, None]
    st2_scs = [None, None]
    st2_st = [None, None]
    st2_ld = [None, None]
    st2_b7 = None

    def st2_load(i):
        it = i % 2
        st2_ld[it] = R.op("sync", (lambda jj, itt: (lambda h: h.dma_start(
            out=st2_in[itt], in_=Y1flat[:, jj * 2048:(jj + 1) * 2048])))(i, it),
            waits=[st2_mm_done[it]], inc="dY2L%d" % it, dma=True)

    st2_slot = [PS[0:64, 3584:4096], PS[0:64, 3072:3584]]

    def emit_st2(k):
        i = k // 2
        half = k % 2
        it = i % 2
        if half == 0 and i + 1 < 128:
            st2_load(i + 1)
        lastsc = None
        for n in (2 * half, 2 * half + 1):
            sl = n % 2
            t_mm = R.op("tensor", (lambda itt, nn, s: (lambda h: h.matmul(
                st2_slot[s], h64[:, :], st2_in[itt][:, nn * 512:(nn + 1) * 512],
                start=True, stop=True)))(it, n, sl),
                waits=([st2_ld[it], st2_scs[sl]] if n == 2 * half else [st2_scs[sl]]) +
                      ([gevs] if k == 0 else []),
                inc="tpe")
            lastsc = R.op("scalar", (lambda nn, hf, s: (lambda h: h.mul(
                st2_out[:, hf * 1024 + (nn % 2) * 512:hf * 1024 + (nn % 2 + 1) * 512],
                st2_slot[s], FWHT_SCALE)))(n, half, sl),
                waits=[t_mm, st2_st[half] if n == 2 * half else None], inc="tac")
            st2_scs[sl] = lastsc
        st2_st[half] = R.op("sync", (lambda jj, hf: (lambda h: h.dma_start(
            out=W2fflat[:, jj * 2048 + hf * 1024:jj * 2048 + (hf + 1) * 1024],
            in_=st2_out[:, hf * 1024:(hf + 1) * 1024])))(i, half),
            waits=[lastsc], inc="dY2W%d" % half, dma=True)
        if half == 1:
            st2_mm_done[it] = lastsc

    st2_load(0)
    oq_free = [None, None]
    xq_free = [None, None]
    x2w_done = [None, None]
    psy_free = None
    normed_free = None
    xn3_free = None
    trC_free = [None, None]
    for qt in range(16):
        emit_st2(qt * 16 + 0)
        emit_st2(qt * 16 + 1)
        p = qt % 2
        oql = None
        for hh in range(16):
            oql = R.op("sync", (lambda hx, pp, qq: (lambda h: h.dma_start(
                out=outq[pp][:, hx, :], in_=outsc[hx, :, qq * 128:(qq + 1) * 128])))(hh, p, qt),
                waits=[oq_free[p]] if hh == 0 else (), inc="dOQ%d" % p, dma=True)
        xql = R.op("sync", (lambda pp, qq: (lambda h: h.dma_start(
            out=xq[pp], in_=xb[qq * 128:(qq + 1) * 128, :])))(p, qt),
            waits=[xq_free[p], t_rinv if qt < 2 else None], inc="dXQ%d" % p, dma=True)
        t_rb = R.op("tensor", (lambda qq: (lambda h: h.matmul(
            PS[0:128, 2304:2432], onesRow[:, :], rinvbf[0:1, qq * 128:(qq + 1) * 128],
            start=True, stop=True)))(qt), waits=[t_rinv, b6_free], inc="tpe")
        t_rbe = R.op("vector", lambda h: h.tensor_copy(rinvb[:, :], PS[0:128, 2304:2432]),
                     waits=[t_rb], inc="tve")
        b6_free = t_rbe
        t_n1 = R.op("vector", (lambda pp: (lambda h: h.tensor_tensor(
            ntmp[:, :, :], outq[pp][:, :, :],
            rinvb[:, :].unsqueeze(1).broadcast_to([128, 16, 128]), ALU.mult)))(p),
            waits=[oql, t_rbe, normed_free], inc="tve")
        t_n2 = R.op("vector", lambda h: h.tensor_tensor(
            normed[:, :, :], ntmp[:, :, :],
            gcol[:, :].unsqueeze(2).broadcast_to([128, 16, 128]), ALU.mult),
            waits=[t_n1, gevs], inc="tve")
        oq_free[p] = t_n1
        last = None
        for i in range(16):
            for nn in range(4):
                last = R.op("tensor", (lambda ii, nb: (lambda h: h.matmul(
                    PS[:, nb * 512:(nb + 1) * 512], normed[:, ii, :],
                    wosb[:, ii, nb * 512:(nb + 1) * 512],
                    start=(ii == 0), stop=(ii == 15))))(i, nn),
                    waits=[t_n2, t_wo, psy_free] if (i == 0 and nn == 0) else (),
                    inc="tpe" if (i == 15 and nn == 3) else None)
            if i < 14:
                emit_st2(qt * 16 + 2 + i)
        normed_free = last
        negm2, rstd2, tst2 = ln_stats(PS[:, 0:2048], 0, [last])
        t_ln2 = R.op("vector", (lambda nm, rs: (lambda h: h.tensor_scalar(
            lnb, PS[:, 0:2048], nm, rs, ALU.add, ALU.mult)))(negm2, rstd2),
            waits=[tst2], inc="tve")
        psy_free = t_ln2
        if dbgY is not None:
            R.op("sync", (lambda qq: (lambda h: h.dma_start(
                out=dbgY[qq * 128:(qq + 1) * 128, :], in_=lnb)))(qt),
                waits=[t_ln2], inc="dDbg", dma=True)
            R.op("sync", (lambda qq: (lambda h: h.dma_start(
                out=dbgNorm[qq].rearrange("p a b -> p (a b)"),
                in_=normed.rearrange("p a b -> p (a b)"))))(qt),
                waits=[t_n2], inc="dDbg", dma=True)
        t_x2 = R.op("vector", (lambda pp: (lambda h: h.tensor_tensor(
            x2b[pp], xq[pp], lnb, ALU.add)))(p),
            waits=[t_ln2, xql, x2w_done[p], gevs if qt < 2 else None,
                   R.tok("dDbg") if dbgY is not None else None], inc="tve")
        xq_free[p] = t_x2
        t_x2w = R.op("gpsimd", (lambda pp, qq: (lambda h: h.dma_start(
            out=x2s[qq * 128:(qq + 1) * 128, :], in_=x2b[pp])))(p, qt),
            waits=[t_x2], inc="dX2W%d" % p, dma=True)
        x2w_done[p] = t_x2w
        negm3, rstd3, tst3 = ln_stats(x2b[p], 32, [t_x2])
        t_xn3 = R.op("vector", (lambda pp, nm, rs: (lambda h: h.tensor_scalar(
            xn3st, x2b[pp], nm, rs, ALU.add, ALU.mult)))(p, negm3, rstd3),
            waits=[tst3, xn3_free], inc="tve")
        for g in range(4):
            bslot = g % 2
            tp = None
            for sub in range(4):
                j = g * 4 + sub
                dst = PSbf[:, (4 + bslot) * 1024 + sub * 128:(4 + bslot) * 1024 + (sub + 1) * 128]
                tp = R.op("tensor", (lambda jj, d: (lambda h: h.transpose(
                    d, xn3st[:, jj * 128:(jj + 1) * 128], ident[:, :])))(j, dst),
                    waits=[t_xn3, trC_free[bslot]] if sub == 0 else (),
                    inc="tpe" if sub == 3 else None)
            if g == 3:
                xn3_free = tp
            ev = R.op("vector", (lambda qq, gg, sl: (lambda h: h.tensor_copy(
                xn3T[:, gg * 4:(gg + 1) * 4, qq * 128:(qq + 1) * 128],
                PSbf[:, (4 + sl) * 1024:(4 + sl) * 1024 + 512].rearrange(
                    "p (a b) -> p a b", a=4, b=128))))(qt, g, bslot),
                waits=[tp, t_rinv if (qt == 0 and g == 0) else None], inc="tve")
            trC_free[bslot] = ev
    R.barrier()

    hidden = rv(R1, 0, (64, 512))
    w1c = [rv(R1, 65536, (16, 128)), rv(R1, 69632, (16, 128))]
    w3c = [rv(R1, 73728, (16, 128)), rv(R1, 77824, (16, 128))]
    w2c = [rv(R1, 81920, (1024,)), rv(R1, 83968, (1024,))]
    slb = [rv(R1, 86016, (512,), F32), rv(R1, 88064, (512,), F32)]
    ybuf = [rv(R1, 90112, (2048,), F32), rv(R1, 98304, (2048,), F32),
            rv(R1, 65536, (2048,), F32), rv(R1, 73728, (2048,), F32)]
    x2q = [rv(R1, 106496, (2048,), F32), rv(R1, 114688, (2048,), F32)]
    lnb4 = rv(R1, 122880, (2048,), F32)


    w1_free = [None, None]
    w13_toks = [None, None]
    w2c_toks = [None, None]
    sil_free = [None, None]
    hid_free = None
    w1c_gate = None
    w3c_gate = None
    out01_gate = None
    w2_free = [None, None]
    x2q_free = [None, None]
    out_done = None
    y_evac = None
    out_toks = []
    for g in range(4):
        last_hid = None

        def load_w13(hcc):
            itt_ = hcc % 2
            R.op("sync", (lambda hx, itt: (lambda h: h.dma_start(
                out=w1c[itt][:, :, :].rearrange("p a b -> p (a b)"),
                in_=W1Ts[hx].rearrange("dw i hh -> dw (i hh)"))))(hcc, itt_),
                waits=[w1_free[itt_], w1c_gate if (hcc < 2 and g > 0) else None],
                inc="dF%d" % itt_, dma=True)
            w13_toks[itt_] = R.op("sync", (lambda hx, itt: (lambda h: h.dma_start(
                out=w3c[itt][:, :, :].rearrange("p a b -> p (a b)"),
                in_=W3Ts[hx].rearrange("dw i hh -> dw (i hh)"))))(hcc, itt_),
                waits=[w3c_gate if (hcc < 2 and g > 0) else None],
                inc="dF%d" % itt_, dma=True)

        load_w13(0)
        for hc in range(64):
            it = hc % 2
            bko = 2 * it
            if hc + 1 < 64:
                load_w13(hc + 1)
            t_l3 = w13_toks[it]
            lastg = None
            for i in range(16):
                lastg = R.op("tensor", (lambda ii, itt, gg, bk: (lambda h: h.matmul(
                    psb(bk), w1c[itt][:, ii, :], xn3T[:, ii, gg * 512:(gg + 1) * 512],
                    start=(ii == 0), stop=(ii == 15))))(i, it, g, bko),
                    waits=[t_l3, sil_free[it],
                           y_evac if (hc < 2 and g > 0) else None] if i == 0 else (),
                    inc="tpe" if i == 15 else None)
            lastl = None
            for i in range(16):
                lastl = R.op("tensor", (lambda ii, itt, gg, bk: (lambda h: h.matmul(
                    psb(bk), w3c[itt][:, ii, :], xn3T[:, ii, gg * 512:(gg + 1) * 512],
                    start=(ii == 0), stop=(ii == 15))))(i, it, g, bko + 1),
                    inc="tpe" if i == 15 else None)
            w1_free[it] = lastl
            t_sil = R.op("scalar", (lambda itt, bk: (lambda h: h.activation(
                slb[itt], psb(bk), AF.Silu)))(it, bko), waits=[lastg], inc="tac")
            last_hid = R.op("vector", (lambda hcc, itt, bk: (lambda h: h.tensor_tensor(
                hidden[:, hcc, :], slb[itt], psb(bk + 1), ALU.mult)))(hc, it, bko),
                waits=[t_sil, lastl, hid_free if hc < 2 else None], inc="tve")
            sil_free[it] = last_hid
        ev4 = [None] * 4
        for dh in range(2):
            lastmm = None

            def load_w2(hcc, dhh):
                itt_ = hcc % 2
                w2c_toks[itt_] = R.op("sync", (lambda hx, dx, itt: (lambda h: h.dma_start(
                    out=w2c[itt], in_=W2fTs[hx * 128:(hx + 1) * 128,
                                            dx * 1024:(dx + 1) * 1024])))(hcc, dhh, itt_),
                    waits=[w2_free[itt_],
                           R.tok("dY2W0") if (g == 0 and dhh == 0 and hcc < 2) else None,
                           R.tok("dY2W1") if (g == 0 and dhh == 0 and hcc < 2) else None],
                    inc="dG%d" % itt_, dma=True)

            load_w2(0, dh)
            for hc in range(64):
                it = hc % 2
                if hc + 1 < 64:
                    load_w2(hc + 1, dh)
                t_l2 = w2c_toks[it]
                for qt4 in range(4):
                    for nn in range(2):
                        ww = ()
                        if qt4 == 0 and nn == 0:
                            ww = [t_l2]
                            if hc == 0:
                                ww.append(last_hid)
                                if dh == 1:
                                    ww.append(ev4[3])
                                elif g > 0:
                                    ww.append(y_evac)
                        lastmm = R.op("tensor", (lambda hcc, itt, q4, nb: (lambda h: h.matmul(
                            PS[:, q4 * 1024 + nb * 512:q4 * 1024 + (nb + 1) * 512],
                            hidden[:, hcc, q4 * 128:(q4 + 1) * 128],
                            w2c[itt][:, nb * 512:(nb + 1) * 512],
                            start=(hcc == 0), stop=(hcc == 63))))(hc, it, qt4, nn),
                            waits=ww, inc="tpe" if (qt4 == 3 and nn == 1) else None)
                w2_free[it] = lastmm
            for qt4 in range(4):
                wv_ = [lastmm]
                if dh == 1:
                    wv_.append(ev4[qt4])
                elif g > 0:
                    wv_.append({2: w1c_gate, 3: w3c_gate}.get(qt4, out01_gate))
                ev4[qt4] = R.op("vector", (lambda q4, dhh: (lambda h: h.tensor_copy(
                    ybuf[q4][:, dhh * 1024:(dhh + 1) * 1024] if False else
                    ybuf[q4].rearrange("p n -> p n")[:, dhh * 1024:(dhh + 1) * 1024],
                    PS[:, q4 * 1024:(q4 + 1) * 1024])))(qt4, dh),
                    waits=wv_, inc="tve")
        hid_free = R.tok("tpe")
        ev4_last = ev4[3]
        ep_out = {}
        for qt4 in (2, 3, 0, 1):
            qt = g * 4 + qt4
            pq = qt4 % 2
            t_x2l = R.op("gpsimd", (lambda qq, pp: (lambda h: h.dma_start(
                out=x2q[pp], in_=x2s[qq * 128:(qq + 1) * 128, :])))(qt, pq),
                waits=[x2q_free[pq]], inc="dX2%d" % pq, dma=True)
            negm4, rstd4, tst4 = ln_stats(ybuf[qt4], 32 * (qt4 % 2), [ev4[qt4]])
            t_ln4 = R.op("vector", (lambda q4, nm, rs: (lambda h: h.tensor_scalar(
                lnb4, ybuf[q4], nm, rs, ALU.add, ALU.mult)))(qt4, negm4, rstd4),
                waits=[tst4], inc="tve")
            t_out = R.op("vector", (lambda q4, pp: (lambda h: h.tensor_tensor(
                ybuf[q4], lnb4, x2q[pp], ALU.add)))(qt4, pq),
                waits=[t_ln4, t_x2l, out_done], inc="tve")
            x2q_free[pq] = t_out
            ep_out[qt4] = t_out
            sem_out = {2: "dOUT2", 3: "dOUT3"}.get(qt4, "dOUT")
            out_done = R.op("gpsimd", (lambda qq, q4: (lambda h: h.dma_start(
                out=yout[qq * 128:(qq + 1) * 128, :], in_=ybuf[q4])))(qt, qt4),
                waits=[t_out], inc=sem_out, dma=True)
            ep_out[qt4] = out_done
            out_toks.append(out_done)
        y_evac = ev4_last
        w1c_gate = ep_out[2]
        w3c_gate = ep_out[3]
        out01_gate = ep_out[1]

    with ctx:
        needed = set(R.cnt.keys())
        for e in ENGS:
            for (fn, ws, inc, amt) in R.ops[e]:
                for s, v in ws:
                    needed.add(s)
        sem_handles = {name: ctx.enter_context(nc.semaphore(name)) for name in sorted(needed)}
        ctx.enter_context(nc.allow_low_precision(
            reason="bf16 matmul operands with fp32 accumulation by design"))
        with nc.Block() as block:
            def mk(engname):
                def runner(h):
                    for fn, ws, inc, amt in R.ops[engname]:
                        for s, v in ws:
                            h.wait_ge(sem_handles[s], v)
                        ins = fn(h)
                        if inc is not None:
                            ins.then_inc(sem_handles[inc], amt)
                    if engname == "gpsimd":
                        for snm in ("dOUT", "dOUT2", "dOUT3"):
                            if snm in R.cnt:
                                h.wait_ge(sem_handles[snm], R.cnt[snm])
                return runner
            block.sync(mk("sync"))
            block.tensor(mk("tensor"))
            block.scalar(mk("scalar"))
            block.vector(mk("vector"))
            block.gpsimd(mk("gpsimd"))
    return nc


_CONSTS = None


def _consts():
    global _CONSTS
    if _CONSTS is None:
        def hadamard(n):
            H = np.array([[1.0]], np.float32)
            while H.shape[0] < n:
                H = np.block([[H, H], [H, -H]])
            return H
        _CONSTS = {
            "cOnesCol": np.ones((128, 1), BFNP),
            "cOnesRow": np.ones((1, 128), BFNP),
            "cOnesRowF": np.ones((1, 128), np.float32),
            "cIdent": np.eye(128, dtype=BFNP),
            "cH128": hadamard(128).astype(BFNP),
            "cH64": hadamard(64).astype(BFNP),
        }
    return _CONSTS


def kernel(L0_attn_x, L0_attn_Wq, L0_attn_Wk, L0_attn_Wv, L0_attn_Wo,
           L0_attn_norm_scale, L0_attn_lam, L0_ffn_W1, L0_ffn_W2, L0_ffn_W3):
    x = np.ascontiguousarray(np.asarray(L0_attn_x, np.float32))
    shared = {
        "Wq": np.ascontiguousarray(L0_attn_Wq, np.float32),
        "Wk": np.ascontiguousarray(L0_attn_Wk, np.float32),
        "Wv": np.ascontiguousarray(L0_attn_Wv, np.float32),
        "Wo": np.ascontiguousarray(L0_attn_Wo, np.float32),
        "W1": np.ascontiguousarray(L0_ffn_W1, np.float32),
        "W2": np.ascontiguousarray(L0_ffn_W2, np.float32),
        "W3": np.ascontiguousarray(L0_ffn_W3, np.float32),
        "gamma": np.ascontiguousarray(np.asarray(L0_attn_norm_scale, np.float32).reshape(1, -1)),
        "lam": np.ascontiguousarray(np.asarray(L0_attn_lam, np.float32).reshape(1, -1)),
    }
    shared.update(_consts())
    in_maps = []
    for c in range(8):
        b, half = c // 2, c % 2
        own = x[b, half * 2048:(half + 1) * 2048]
        oth = x[b, (1 - half) * 2048:(2 - half) * 2048]
        m = dict(shared)
        m["xb"] = np.ascontiguousarray(np.concatenate([own, oth], 0))
        in_maps.append(m)
    nc = build_program()
    res = run_bass_kernel_spmd(nc, in_maps, core_ids=list(range(8)))
    out = np.empty((4, 4096, 2048), np.float32)
    for c in range(8):
        b, half = c // 2, c % 2
        out[b, half * 2048:(half + 1) * 2048] = np.asarray(res.results[c]["yout"], np.float32)
    return out
